# version 05 (final)
# baseline (speedup 1.0000x reference)
import contextlib
import numpy as np
import ml_dtypes
import concourse.bass as bass
import concourse.mybir as mybir
from concourse.bass_utils import run_bass_kernel_spmd

F32 = mybir.dt.float32
BF16 = mybir.dt.bfloat16
AF = mybir.ActivationFunctionType
ALU = mybir.AluOpType
AX = mybir.AxisListType
BF = ml_dtypes.bfloat16

RACE_CHECK = [True]


def _bass():
    return bass.Bass("TRN2", target_bir_lowering=False, debug=False,
                     detect_race_conditions=RACE_CHECK[0])

NCORES = 8
T = 1024
D = 2048
DC = D // 128
DFF = 5632
FC = DFF // 128
S = 512
NS = T // S

ENGS = ("sync", "scalar", "vector", "tensor", "gpsimd")


class Sem:
    def __init__(self, handle):
        self.h = handle
        self.n = 0

    def inc(self, k=1):
        self.n += k
        return self.n


class Prog:
    def __init__(self, nc):
        self.nc = nc
        self.q = {e: [] for e in ENGS}
        self.ctx = contextlib.ExitStack()

    def sem(self, name):
        return Sem(self.ctx.enter_context(self.nc.semaphore(name)))

    def sb(self, name, shape, dt, stack=None):
        return (stack or self.ctx).enter_context(self.nc.sbuf_tensor(name, shape, dt))

    def ps(self, name, shape, dt=F32, stack=None):
        return (stack or self.ctx).enter_context(self.nc.psum_tensor(name, shape, dt))

    def add(self, eng, fn):
        self.q[eng].append(fn)

    def run(self):
        with self.nc.Block() as block:
            for e in ENGS:
                fns = self.q[e]
                if not fns:
                    continue
                getattr(block, e)(lambda eng, fns=fns: [f(eng) for f in fns])
        self.ctx.close()


class Stream:

    def __init__(self, p, name, nsems):
        self.sems = [p.sem(f"{name}{i}") for i in range(nsems)]
        self.nsems = nsems
        self.k = 0

    def tok(self, ndma=1):
        s = self.sems[self.k % self.nsems]
        v = s.inc(16 * ndma)
        self.k += 1
        return (s, v)

    def wait(self, eng, t):
        eng.wait_ge(t[0].h, t[1])

    def inc(self, ins, t):
        ins.then_inc(t[0].h, 16)

    def barrier(self, eng):
        for s in self.sems:
            eng.wait_ge(s.h, s.n)


def wtok(eng, t):
    eng.wait_ge(t[0].h, t[1])



def build_prep():
    nc = _bass()
    w1s = nc.dram_tensor("w1s", [704, 2048], F32, kind="ExternalInput").ap()
    w2s = nc.dram_tensor("w2s", [704, 2048], F32, kind="ExternalInput").ap()
    w3s = nc.dram_tensor("w3s", [256, 5632], F32, kind="ExternalInput").ap()
    uslice = nc.dram_tensor("uslice", [1024, 2048], F32, kind="ExternalInput").ap()
    router = nc.dram_tensor("router", [8, 2048], F32, kind="ExternalInput").ap()
    identf = nc.dram_tensor("identf", [128, 128], F32, kind="ExternalInput").ap()
    cast_shapes = dict(
        Win=[1044, 2048], Wout=[256, 4096], expert=[2048, 2048],
        cq=[256, 768], qn=[96, 1024], qr=[96, 512], ckv=[256, 256],
        kn=[32, 1024], vv=[32, 1024], kr=[256, 32], proj=[128, 2048],
    )
    casts = {}
    for nm, shp in cast_shapes.items():
        casts[nm] = (
            nc.dram_tensor(nm + "_in", shp, F32, kind="ExternalInput").ap(),
            nc.dram_tensor(nm + "_bf", shp, BF16, kind="ExternalOutput").ap(),
        )
    w1q = nc.dram_tensor("w1q", [704, 2048], BF16, kind="ExternalOutput").ap()
    w2q = nc.dram_tensor("w2q", [704, 2048], BF16, kind="ExternalOutput").ap()
    w3q = nc.dram_tensor("w3q", [256, 5632], BF16, kind="ExternalOutput").ap()
    routerT = nc.dram_tensor("routerT", [D, 8], BF16, kind="ExternalOutput").ap()
    xst0 = nc.dram_tensor("xst0", [D, T], F32, kind="ExternalOutput").ap()
    ar_in = nc.dram_tensor("ar_in", [1, 4], F32, kind="Internal").ap()
    ar_out = nc.dram_tensor("ar_out", [1, 4], F32, kind="Internal").ap()

    p = Prog(nc)
    sV = p.sem("sV")
    sP = p.sem("sP")
    sC = p.sem("sC")
    ld1 = Stream(p, "ld1", 1)
    gld = Stream(p, "gld", 2)
    qld = Stream(p, "qld", 2)
    qst = Stream(p, "qst", 2)
    rst = Stream(p, "rst", 1)
    ust = Stream(p, "ust", 1)
    gio = Stream(p, "gio", 1)
    cst = Stream(p, "cst", 1)

    wla = p.sb("wla", [128, 5632], F32)
    wlb = p.sb("wlb", [128, 5632], F32)
    acc = p.sb("acc", [128, 16], F32)
    onesc = p.sb("onesc", [128, 1], F32)
    onesr = p.sb("onesr", [1, 128], F32)
    idt = p.sb("idt", [128, 128], F32)
    grow = p.sb("grow", [1, 4], F32)
    thc = p.sb("thc", [128, 8], F32)
    mpos = p.sb("mpos", [128, 2 * 5632], BF16)
    mneg = p.sb("mneg", [128, 2 * 5632], BF16)
    qout = p.sb("qout", [128, 2 * 5632], BF16)
    rtr = p.sb("rtr", [8, 2048], F32)
    rts = p.sb("rts", [128, 8], BF16)
    ut = p.sb("ut", [128, 8 * 2048], F32)
    stg = p.sb("stg", [128, 1024], F32)
    gps = p.ps("gps", [1, 16], F32)
    gbc = p.ps("gbc", [128, 4], F32)
    rtp = p.ps("rtp", [128, 8], F32)
    utp = [p.ps(f"utp{i}", [128, 128], F32) for i in range(2)]

    mats = [(w1s, w1q, 704, 2048, 0), (w2s, w2q, 704, 2048, 1),
            (w3s, w3q, 256, 5632, 2)]
    tiles = []
    for msrc, mdst, rows, cols, j in mats:
        for r0 in range(0, rows, 128):
            tiles.append((msrc, mdst, r0, min(128, rows - r0), cols, j))
    NTL = len(tiles)

    for _ in range(10):
        ld1.tok()
    ld1_all = (ld1.sems[0], ld1.sems[0].n)
    gtok = [gld.tok() for _ in range(NTL)]
    qtok = [qld.tok() for _ in range(NTL)]
    stok = [qst.tok() for _ in range(NTL)]
    rtok = [rst.tok() for _ in range(16)]
    utok = [ust.tok() for _ in range(DC)]
    g_arin = gio.tok()
    ar_done = sC.inc(1)
    g_arout = gio.tok()
    for _ in casts:
        cst.tok()
    gv = [sV.inc(1) for _ in range(NTL)]
    ve_grow = sV.inc()
    ve_th = sV.inc()
    qv = [sV.inc(3) for _ in range(NTL)]
    rt_ve = [sV.inc() for _ in range(16)]
    ut_ve = [sV.inc() for _ in range(128)]
    pe_g = sP.inc()
    pe_gbc = sP.inc()
    rt_pe = [sP.inc() for _ in range(16)]
    ut_pe = [sP.inc() for _ in range(128)]

    def g_all(g):
        for nm in casts:
            srcx, dst = casts[nm]
            cstt = ctoks.pop(0)
            g.dma_start(out=dst, in_=srcx).then_inc(cstt[0].h, 16)
        g.wait_ge(sV.h, ve_grow)
        g.dma_start(out=ar_in, in_=grow[:]).then_inc(g_arin[0].h, 16)
        wtok(g, g_arin)
        g.collective_compute("AllReduce", ALU.add,
                             replica_groups=[list(range(NCORES))],
                             ins=[ar_in], outs=[ar_out]).then_inc(sC.h, 1)
        g.wait_ge(sC.h, ar_done)
        g.dma_start(out=grow[:], in_=ar_out).then_inc(g_arout[0].h, 16)
        for i, (msrc, mdst, r0, nr, cols, j) in enumerate(tiles):
            o = (i % 2) * 5632
            g.wait_ge(sV.h, qv[i])
            g.dma_start(out=mdst[r0:r0 + nr, :],
                        in_=qout[0:nr, o:o + cols]).then_inc(stok[i][0].h, 16)
    ctoks = [cst.sems[0]] and []
    for _ in casts:
        pass
    ctoks = []
    kk = 0
    for _ in casts:
        ctoks.append((cst.sems[0], 16 * (kk + 1)))
        kk += 1
    p.add("gpsimd", g_all)

    def s_all(s):
        s.dma_start(out=idt[:], in_=identf).then_inc(ld1.sems[0].h, 16)
        for i in range(8):
            s.dma_start(out=ut[:, i * 2048:(i + 1) * 2048],
                        in_=uslice[i * 128:(i + 1) * 128, :]).then_inc(ld1.sems[0].h, 16)
        s.dma_start(out=rtr[:], in_=router).then_inc(ld1.sems[0].h, 16)
        for i, (msrc, mdst, r0, nr, cols, j) in enumerate(tiles):
            buf = wla if i % 2 == 0 else wlb
            if i >= 2:
                s.wait_ge(sV.h, gv[i - 2])
            s.dma_start(out=buf[0:nr, 0:cols],
                        in_=msrc[r0:r0 + nr, :]).then_inc(gtok[i][0].h, 16)
        for i, (msrc, mdst, r0, nr, cols, j) in enumerate(tiles):
            buf = wla if i % 2 == 0 else wlb
            s.wait_ge(sV.h, ve_th if i < 2 else qv[i - 2])
            s.dma_start(out=buf[0:nr, 0:cols],
                        in_=msrc[r0:r0 + nr, :]).then_inc(qtok[i][0].h, 16)
        for i in range(16):
            s.wait_ge(sV.h, rt_ve[i])
            s.dma_start(out=routerT[i * 128:(i + 1) * 128, :],
                        in_=rts[:, 0:8]).then_inc(rtok[i][0].h, 16)
        for d in range(DC):
            s.wait_ge(sV.h, ut_ve[d * 8 + 7])
            s.dma_start(out=xst0[d * 128:(d + 1) * 128, :],
                        in_=stg[:]).then_inc(utok[d][0].h, 16)
        for st in (ld1, gld, qld, qst, rst, ust, gio, cst):
            st.barrier(s)
        s.wait_ge(sV.h, sV.n)
    p.add("sync", s_all)

    def v_all(v):
        v.memset(onesc[:], 1.0)
        v.memset(onesr[:], 1.0)
        v.memset(grow[:], 0.0)
        v.memset(acc[:], 0.0)
        for i, (msrc, mdst, r0, nr, cols, j) in enumerate(tiles):
            buf = wla if i % 2 == 0 else wlb
            wtok(v, gtok[i])
            v.reduce_sum(out=acc[0:nr, i:i + 1], in_=buf[0:nr, 0:cols],
                         axis=AX.X, apply_absolute_value=True).then_inc(sV.h, 1)
        v.wait_ge(sP.h, pe_g)
        mranges = [(0, 6), (6, 12), (12, 14)]
        last = None
        for j, (a, b) in enumerate(mranges):
            last = v.reduce_sum(out=grow[0:1, j:j + 1], in_=gps[0:1, a:b],
                                axis=AX.X)
        last.then_inc(sV.h, 1)
        v.wait_ge(sP.h, pe_gbc)
        Nels = [5632.0 * 2048, 5632.0 * 2048, 2048.0 * 5632]
        last = None
        for j in range(3):
            v.tensor_scalar(out=thc[:, j:j + 1], in0=gbc[:, j:j + 1],
                            scalar1=0.5 / Nels[j], scalar2=0.5e-8,
                            op0=ALU.mult, op1=ALU.add)
            last = v.tensor_scalar(out=thc[:, 4 + j:5 + j], in0=gbc[:, j:j + 1],
                                   scalar1=-0.5 / Nels[j], scalar2=-0.5e-8,
                                   op0=ALU.mult, op1=ALU.add)
        last.then_inc(sV.h, 1)
        for i, (msrc, mdst, r0, nr, cols, j) in enumerate(tiles):
            buf = wla if i % 2 == 0 else wlb
            o = (i % 2) * 5632
            wtok(v, qtok[i])
            if i >= 2:
                wtok(v, stok[i - 2])
            v.tensor_scalar(out=mpos[0:nr, o:o + cols], in0=buf[0:nr, 0:cols],
                            scalar1=thc[0:nr, j:j + 1], scalar2=None, op0=ALU.is_gt)
            v.tensor_scalar(out=mneg[0:nr, o:o + cols], in0=buf[0:nr, 0:cols],
                            scalar1=thc[0:nr, 4 + j:5 + j], scalar2=None, op0=ALU.is_lt)
            v.tensor_sub(out=qout[0:nr, o:o + cols], in0=mpos[0:nr, o:o + cols],
                         in1=mneg[0:nr, o:o + cols]).then_inc(sV.h, 3)
        for i in range(16):
            v.wait_ge(sP.h, rt_pe[i])
            if i >= 1:
                wtok(v, rtok[i - 1])
            v.tensor_copy(out=rts[:, 0:8], in_=rtp[:]).then_inc(sV.h, 1)
        for d in range(DC):
            for si in range(8):
                i = d * 8 + si
                if si == 0 and d >= 1:
                    wtok(v, utok[d - 1])
                v.wait_ge(sP.h, ut_pe[i])
                v.tensor_copy(out=stg[:, si * 128:(si + 1) * 128],
                              in_=utp[i % 2][:]).then_inc(sV.h, 1)
    p.add("vector", v_all)

    def t_all(t):
        t.wait_ge(sV.h, gv[NTL - 1])
        t.matmul(gps[:], lhsT=onesc[:], rhs=acc[:, 0:16], start=True,
                 stop=True).then_inc(sP.h, 1)
        wtok(t, g_arout)
        t.matmul(gbc[:], lhsT=onesr[:], rhs=grow[:], start=True,
                 stop=True).then_inc(sP.h, 1)
        wtok(t, ld1_all)
        for i in range(16):
            if i >= 1:
                t.wait_ge(sV.h, rt_ve[i - 1])
            t.transpose(out=rtp[:], in_=rtr[0:8, i * 128:(i + 1) * 128],
                        identity=idt[0:8, 0:8]).then_inc(sP.h, 1)
        for d in range(DC):
            for si in range(8):
                i = d * 8 + si
                if i >= 2:
                    t.wait_ge(sV.h, ut_ve[i - 2])
                t.transpose(out=utp[i % 2][:],
                            in_=ut[:, si * 2048 + d * 128:si * 2048 + (d + 1) * 128],
                            identity=idt[:]).then_inc(sP.h, 1)
    p.add("tensor", t_all)

    p.run()
    return nc



def build_ffn(coef=0.5):
    nc = _bass()
    xst = nc.dram_tensor("xst", [D, T], F32, kind="ExternalInput").ap()
    w1q = nc.dram_tensor("w1q", [DFF, D], BF16, kind="ExternalInput").ap()
    w2q = nc.dram_tensor("w2q", [DFF, D], BF16, kind="ExternalInput").ap()
    w3q = nc.dram_tensor("w3q", [D, DFF], BF16, kind="ExternalInput").ap()
    xout = nc.dram_tensor("xout", [D, T], F32, kind="ExternalOutput").ap()

    p = Prog(nc)
    sE = p.sem("sE")
    sV = p.sem("sV")
    sP = p.sem("sP")
    xs_ = Stream(p, "xs", 2)
    wa = Stream(p, "wa", 2)
    wb = Stream(p, "wb", 2)
    wc = Stream(p, "wc", 2)
    xr = Stream(p, "xr", 4)
    ot = Stream(p, "ot", 4)

    xftmp = p.sb("xftmp", [128, 2 * 512], F32)
    xlnb = p.sb("xlnb", [128, DC * 512], BF16)
    xsq = p.sb("xsq", [128, 2 * 512], BF16)
    srow = p.sb("srow", [1, 3 * 512], F32)
    onesb = p.sb("onesb", [128, 1], BF16)
    onesrf = p.sb("onesrf", [1, 128], F32)
    epsc = p.sb("epsc", [128, 1], F32)
    hbuf = p.sb("hbuf", [128, FC * 512], BF16)
    w1t = p.sb("w1t", [128, 2 * 16 * 512], BF16)
    w2t = p.sb("w2t", [128, 2 * 16 * 512], BF16)
    w3t = p.sb("w3t", [128, 2 * 512], BF16)
    sgb = p.sb("sgb", [128, 2 * 512], BF16)
    xres = p.sb("xres", [128, 4 * 512], F32)
    xot = p.sb("xot", [128, 4 * 512], F32)

    lnst = contextlib.ExitStack()
    psS1 = p.ps("psS1", [1, 512], F32, stack=lnst)
    psS2 = p.ps("psS2", [1, 512], F32, stack=lnst)
    mbc = p.ps("mbc", [128, 512], F32, stack=lnst)
    rbc = p.ps("rbc", [128, 512], F32, stack=lnst)
    lnst.close()
    x1p = [p.ps(f"x1p{a}", [128, 512], F32) for a in range(2)]
    x2p = [p.ps(f"x2p{a}", [128, 512], F32) for a in range(2)]
    yp = [p.ps(f"yp{d4}", [128, 512], F32) for d4 in range(4)]

    NP = 11

    C = {}
    for h in range(2):
        c = {}
        c["xld"] = [xs_.tok() for _ in range(DC)]
        c["xrld"] = [xr.tok() for _ in range(4)]
        c["sti"] = []
        c["xrld2"] = []
        for i in range(DC):
            c["sti"].append(ot.tok())
            if i + 4 < DC:
                c["xrld2"].append(xr.tok())
        c["xrld"] += c["xrld2"]
        c["w1ld"] = [None] * NP
        c["w2ld"] = [None] * NP
        c["w3ld"] = [[None] * FC for _ in range(4)]
        c["cast"] = [sE.inc() for _ in range(DC)]
        c["sq"] = [sE.inc() for _ in range(DC)]
        c["sd"] = sE.inc()
        c["sg"] = [sE.inc() for _ in range(FC)]
        if h == 0:
            c["init"] = sV.inc()
        c["var"] = sV.inc()
        c["rstd"] = sV.inc()
        c["xln"] = sV.inc()
        c["h"] = [sV.inc() for _ in range(FC)]
        c["yev"] = [sV.inc() for _ in range(DC)]
        c["stats"] = [sP.inc(2) for _ in range(DC)]
        c["bc"] = sP.inc()
        c["mmx"] = [sP.inc() for _ in range(FC)]
        c["pe_panel"] = [c["mmx"][mp * 4 + 3] for mp in range(NP)]
        c["w3pe"] = [[0] * FC for _ in range(4)]
        c["ype"] = [0] * 4
        for dp in range(4):
            for k in range(FC):
                c["w3pe"][dp][k] = sP.inc()
            c["ype"][dp] = c["w3pe"][dp][FC - 1]
        C[h] = c

    for h in range(2):
        if h == 0:
            for q in (0, 1):
                C[q // NP]["w1ld"][q % NP] = wa.tok(16)
                C[q // NP]["w2ld"][q % NP] = wb.tok(16)
        for m in range(FC):
            if m % 4 == 3:
                q = h * NP + m // 4 + 2
                if q < 2 * NP:
                    C[q // NP]["w1ld"][q % NP] = wa.tok(16)
                    C[q // NP]["w2ld"][q % NP] = wb.tok(16)
        for dp in range(4):
            for k in range(FC):
                C[h]["w3ld"][dp][k] = wc.tok()

    w3pe_flat = []
    for h in range(2):
        for dp in range(4):
            w3pe_flat += C[h]["w3pe"][dp]

    def s_all(s):
        for h in range(2):
            c = C[h]
            for d in range(DC):
                gi = h * DC + d
                if gi >= 2:
                    hh, dd = (gi - 2) // DC, (gi - 2) % DC
                    s.wait_ge(sE.h, C[hh]["cast"][dd])
                s.dma_start(out=xftmp[:, (gi % 2) * 512:(gi % 2 + 1) * 512],
                            in_=xst[d * 128:(d + 1) * 128, h * 512:(h + 1) * 512]
                            ).then_inc(c["xld"][d][0].h, 16)
            for i in range(4):
                s.dma_start(out=xres[:, (i % 4) * 512:(i % 4 + 1) * 512],
                            in_=xst[i * 128:(i + 1) * 128, h * 512:(h + 1) * 512]
                            ).then_inc(c["xrld"][i][0].h, 16)
            for i in range(DC):
                s.wait_ge(sV.h, c["yev"][i])
                s.dma_start(out=xout[i * 128:(i + 1) * 128, h * 512:(h + 1) * 512],
                            in_=xot[:, (i % 4) * 512:(i % 4 + 1) * 512]
                            ).then_inc(c["sti"][i][0].h, 16)
                if i + 4 < DC:
                    i2 = i + 4
                    s.dma_start(out=xres[:, (i2 % 4) * 512:(i2 % 4 + 1) * 512],
                                in_=xst[i2 * 128:(i2 + 1) * 128,
                                        h * 512:(h + 1) * 512]
                                ).then_inc(c["xrld"][i2][0].h, 16)
        for st in (xs_, wa, wb, wc, xr, ot):
            st.barrier(s)
        s.wait_ge(sV.h, sV.n)
    p.add("sync", s_all)

    def e_all(sc):
        for h in range(2):
            c = C[h]
            for d in range(DC):
                gi = h * DC + d
                wtok(sc, c["xld"][d])
                if h > 0 and d == 0:
                    sc.wait_ge(sP.h, C[h - 1]["mmx"][FC - 1])
                sc.copy(out=xlnb[:, d * 512:(d + 1) * 512],
                        in_=xftmp[:, (gi % 2) * 512:(gi % 2 + 1) * 512]
                        ).then_inc(sE.h, 1)
            for d in range(DC):
                gi = h * DC + d
                if gi >= 2:
                    hh, dd = (gi - 2) // DC, (gi - 2) % DC
                    sc.wait_ge(sP.h, C[hh]["stats"][dd])
                sc.activation(out=xsq[:, (gi % 2) * 512:(gi % 2 + 1) * 512],
                              in_=xlnb[:, d * 512:(d + 1) * 512],
                              func=AF.Square).then_inc(sE.h, 1)
            sc.wait_ge(sV.h, c["var"])
            sc.activation(out=srow[0:1, 1024:1536], in_=srow[0:1, 512:1024],
                          func=AF.Sqrt, bias=epsc[0:1, 0:1]).then_inc(sE.h, 1)
            if h == 0:
                for mp in range(2):
                    for k in range(16):
                        sc.dma_start(
                            out=w1t[:, (mp * 16 + k) * 512:(mp * 16 + k + 1) * 512],
                            in_=w1q[mp * 512:(mp + 1) * 512, k * 128:(k + 1) * 128],
                            transpose=True).then_inc(C[0]["w1ld"][mp][0].h, 16)
                    for k in range(16):
                        sc.dma_start(
                            out=w2t[:, (mp * 16 + k) * 512:(mp * 16 + k + 1) * 512],
                            in_=w2q[mp * 512:(mp + 1) * 512, k * 128:(k + 1) * 128],
                            transpose=True).then_inc(C[0]["w2ld"][mp][0].h, 16)
            for m in range(FC):
                a = m % 2
                sc.wait_ge(sP.h, c["mmx"][m])
                if m >= 2:
                    sc.wait_ge(sV.h, c["h"][m - 2])
                elif h > 0:
                    sc.wait_ge(sV.h, C[h - 1]["h"][FC - 2 + m])
                sc.activation(out=sgb[:, a * 512:(a + 1) * 512],
                              in_=x1p[a][:], func=AF.Sigmoid).then_inc(sE.h, 1)
                if m % 4 == 3:
                    mp = m // 4
                    q = h * NP + mp + 2
                    if q < 2 * NP:
                        h2, mp2 = q // NP, q % NP
                        pq = q % 2
                        for k in range(16):
                            sc.dma_start(
                                out=w1t[:, (pq * 16 + k) * 512:(pq * 16 + k + 1) * 512],
                                in_=w1q[mp2 * 512:(mp2 + 1) * 512,
                                        k * 128:(k + 1) * 128],
                                transpose=True).then_inc(C[h2]["w1ld"][mp2][0].h, 16)
                        for k in range(16):
                            sc.dma_start(
                                out=w2t[:, (pq * 16 + k) * 512:(pq * 16 + k + 1) * 512],
                                in_=w2q[mp2 * 512:(mp2 + 1) * 512,
                                        k * 128:(k + 1) * 128],
                                transpose=True).then_inc(C[h2]["w2ld"][mp2][0].h, 16)
            for dp in range(4):
                for k in range(FC):
                    gi = (h * 4 + dp) * FC + k
                    if gi >= 2:
                        sc.wait_ge(sP.h, w3pe_flat[gi - 2])
                    sc.dma_start(out=w3t[:, (k % 2) * 512:(k % 2 + 1) * 512],
                                 in_=w3q[dp * 512:(dp + 1) * 512,
                                         k * 128:(k + 1) * 128],
                                 transpose=True).then_inc(C[h]["w3ld"][dp][k][0].h, 16)
    p.add("scalar", e_all)

    def v_all(v):
        for h in range(2):
            c = C[h]
            if h == 0:
                v.memset(onesb[:], 1.0)
                v.memset(epsc[:], 1e-5)
                v.memset(onesrf[:], 1.0).then_inc(sV.h, 1)
            v.wait_ge(sP.h, c["stats"][DC - 1])
            mean = srow[0:1, 0:512]
            s1 = srow[0:1, 512:1024]
            s2 = srow[0:1, 1024:1536]
            v.tensor_scalar(out=mean, in0=psS1[:], scalar1=1.0 / D,
                            scalar2=None, op0=ALU.mult)
            v.tensor_scalar(out=s1, in0=psS2[:], scalar1=1.0 / D,
                            scalar2=None, op0=ALU.mult)
            v.tensor_mul(out=s2, in0=mean, in1=mean)
            v.tensor_sub(out=s1, in0=s1, in1=s2).then_inc(sV.h, 1)
            v.wait_ge(sE.h, c["sd"])
            v.reciprocal(out=srow[0:1, 512:1024],
                         in_=srow[0:1, 1024:1536]).then_inc(sV.h, 1)
            v.wait_ge(sP.h, c["bc"])
            last = None
            for d in range(DC):
                gi = h * DC + d
                tmp = xftmp[:, (gi % 2) * 512:(gi % 2 + 1) * 512]
                v.tensor_sub(out=tmp, in0=xlnb[:, d * 512:(d + 1) * 512], in1=mbc[:])
                last = v.tensor_mul(out=xlnb[:, d * 512:(d + 1) * 512],
                                    in0=tmp, in1=rbc[:])
            last.then_inc(sV.h, 1)
            for m in range(FC):
                a = m % 2
                v.wait_ge(sE.h, c["sg"][m])
                v.tensor_mul(out=hbuf[:, m * 512:(m + 1) * 512],
                             in0=sgb[:, a * 512:(a + 1) * 512],
                             in1=x2p[a][:]).then_inc(sV.h, 1)
            for i in range(DC):
                dp, d4 = i // 4, i % 4
                v.wait_ge(sP.h, c["ype"][dp])
                wtok(v, c["xrld"][i])
                if i >= 4:
                    wtok(v, c["sti"][i - 4])
                elif h > 0:
                    wtok(v, C[h - 1]["sti"][DC - 4 + i])
                v.scalar_tensor_tensor(out=xot[:, (i % 4) * 512:(i % 4 + 1) * 512],
                                       in0=yp[d4][:], scalar=coef,
                                       in1=xres[:, (i % 4) * 512:(i % 4 + 1) * 512],
                                       op0=ALU.mult, op1=ALU.add).then_inc(sV.h, 1)
    p.add("vector", v_all)

    def t_all(t):
        for h in range(2):
            c = C[h]
            if h == 0:
                t.wait_ge(sV.h, c["init"])
            for d in range(DC):
                gi = h * DC + d
                t.wait_ge(sE.h, c["sq"][d])
                t.matmul(psS1[:], lhsT=onesb[:],
                         rhs=xlnb[:, d * 512:(d + 1) * 512],
                         start=(d == 0), stop=(d == DC - 1))
                t.matmul(psS2[:], lhsT=onesb[:],
                         rhs=xsq[:, (gi % 2) * 512:(gi % 2 + 1) * 512],
                         start=(d == 0), stop=(d == DC - 1)).then_inc(sP.h, 2)
            t.wait_ge(sV.h, c["rstd"])
            t.matmul(mbc[:], lhsT=onesrf[:], rhs=srow[0:1, 0:512],
                     start=True, stop=True)
            t.matmul(rbc[:], lhsT=onesrf[:], rhs=srow[0:1, 512:1024],
                     start=True, stop=True).then_inc(sP.h, 1)
            t.wait_ge(sV.h, c["xln"])
            for mp in range(NP):
                wtok(t, c["w1ld"][mp])
                wtok(t, c["w2ld"][mp])
                pq = (h * NP + mp) % 2
                for m4 in range(4):
                    m = mp * 4 + m4
                    a = m % 2
                    if m >= 2:
                        t.wait_ge(sV.h, c["h"][m - 2])
                    elif h > 0:
                        t.wait_ge(sV.h, C[h - 1]["h"][FC - 2 + m])
                    for k in range(16):
                        t.matmul(x1p[a][:],
                                 lhsT=w1t[:, (pq * 16 + k) * 512 + m4 * 128:
                                          (pq * 16 + k) * 512 + (m4 + 1) * 128],
                                 rhs=xlnb[:, k * 512:(k + 1) * 512],
                                 start=(k == 0), stop=(k == 15))
                    last = None
                    for k in range(16):
                        last = t.matmul(
                            x2p[a][:],
                            lhsT=w2t[:, (pq * 16 + k) * 512 + m4 * 128:
                                     (pq * 16 + k) * 512 + (m4 + 1) * 128],
                            rhs=xlnb[:, k * 512:(k + 1) * 512],
                            start=(k == 0), stop=(k == 15))
                    last.then_inc(sP.h, 1)
            t.wait_ge(sV.h, c["h"][FC - 1])
            for dp in range(4):
                for k in range(FC):
                    wtok(t, c["w3ld"][dp][k])
                    t.wait_ge(sV.h, c["h"][k])
                    if k == 0 and (h > 0 or dp > 0):
                        pi = (h * 4 + dp) - 1
                        hh, dd = pi // 4, pi % 4
                        for d4 in range(4):
                            t.wait_ge(sV.h, C[hh]["yev"][dd * 4 + d4])
                    last = None
                    for d4 in range(4):
                        last = t.matmul(
                            yp[d4][:],
                            lhsT=w3t[:, (k % 2) * 512 + d4 * 128:
                                     (k % 2) * 512 + (d4 + 1) * 128],
                            rhs=hbuf[:, k * 512:(k + 1) * 512],
                            start=(k == 0), stop=(k == FC - 1))
                    last.then_inc(sP.h, 1)
    p.add("tensor", t_all)

    p.run()
    return nc



def build_moe(coef=0.5, final_out=False):
    nc = _bass()
    xst = nc.dram_tensor("xst", [D, T], F32, kind="ExternalInput").ap()
    experts = nc.dram_tensor("experts", [8 * D, D], BF16, kind="ExternalInput").ap()
    routerT = nc.dram_tensor("routerT", [D, 8], BF16, kind="ExternalInput").ap()
    identf = nc.dram_tensor("identf", [128, 128], F32, kind="ExternalInput").ap()
    if final_out:
        xout = nc.dram_tensor("xout", [T, D], F32, kind="ExternalOutput").ap()
    else:
        xout = nc.dram_tensor("xout", [D, T], F32, kind="ExternalOutput").ap()

    p = Prog(nc)
    sE = p.sem("sE")
    sV = p.sem("sV")
    sP = p.sem("sP")
    sZ = p.sem("sZ")
    zc = [0]

    def zsync(v, ins):
        zc[0] += 1
        ins.then_inc(sZ.h, 1)
        v.wait_ge(sZ.h, zc[0])

    xs_ = Stream(p, "xs", 2)
    rt_s = Stream(p, "rt", 1)
    we = Stream(p, "we", 2)
    xr = Stream(p, "xr", 4)
    st = Stream(p, "st", 1)
    si_s = Stream(p, "si", 1)

    xftmp = p.sb("xftmp", [128, 2 * 1024], F32)
    xlnb = p.sb("xlnb", [128, DC * 1024], BF16)
    xsq = p.sb("xsq", [128, 2 * 1024], BF16)
    srow = p.sb("srow", [1, 6 * 512], F32)
    onesb = p.sb("onesb", [128, 1], BF16)
    onesrf = p.sb("onesrf", [1, 128], F32)
    epsc = p.sb("epsc", [128, 1], F32)
    idt = p.sb("idt", [128, 128], F32)
    rt = p.sb("rt", [128, DC * 8], BF16)
    lgs = p.sb("lgs", [8, 1024], F32)
    masks = p.sb("masks", [128, 8 * 9], F32)
    wet = p.sb("wet", [128, 2 * 16 * 512], BF16)
    accb = p.sb("accb", [128, 8 * 512], F32)
    outst = p.sb("outst", [128, 4 * 1024], F32)
    xres = p.sb("xres", [128, 4 * 1024], F32)

    lnst = contextlib.ExitStack()
    psS1 = [p.ps(f"psS1{s}", [1, 512], F32, stack=lnst) for s in range(NS)]
    psS2 = [p.ps(f"psS2{s}", [1, 512], F32, stack=lnst) for s in range(NS)]
    mbc = p.ps("mbc", [128, 1024], F32, stack=lnst)
    rbc = p.ps("rbc", [128, 1024], F32, stack=lnst)
    lnst.close()
    lps = p.ps("lps", [8, 1024], F32)
    ltp = p.ps("ltp", [128, 8], F32)
    xpp = [p.ps(f"xp{a}", [128, 512], F32) for a in range(2)]
    ttp = [p.ps(f"ttp{a}", [128, 128], F32) for a in range(2)]

    xld = [xs_.tok() for _ in range(DC)]
    rtld = [rt_s.tok() for _ in range(DC)]
    rt_all = (rt_s.sems[0], rt_s.sems[0].n)
    xrld = [[None] * 4 for _ in range(4)]
    st_i = [[None] * 8 for _ in range(4)]
    for ft in range(4):
        for j in range(4):
            xrld[ft][j] = xr.tok()
        nst = 8 if final_out else 4
        for s in range(nst):
            st_i[ft][s] = st.tok()
    weld = [[None] * 8 for _ in range(4)]
    for ft in range(4):
        for e in range(8):
            weld[ft][e] = we.tok(16)
    idld = si_s.tok()
    cast_done = [sE.inc() for _ in range(DC)]
    sq_done = [sE.inc() for _ in range(DC)]
    sd_done = sE.inc()
    init_done = sV.inc()
    var_done = sV.inc()
    rstd_done = sV.inc()
    xln_done = sV.inc()
    lgs_done = sV.inc()
    msk_done = [sV.inc() for _ in range(8)]
    ev = [[[0] * 8 for _ in range(8)] for _ in range(4)]
    oev = [[0] * 32 for _ in range(4)]
    for ft in range(4):
        for e in range(8):
            for s in range(8):
                ev[ft][e][s] = sV.inc()
        for i in range(32):
            oev[ft][i] = sV.inc()
    stats_done = [sP.inc(4) for _ in range(DC)]
    bc_done = sP.inc()
    lg_done = sP.inc()
    ltp_done = [sP.inc() for _ in range(8)]
    mm_e = [[[0] * 8 for _ in range(8)] for _ in range(4)]
    tr_done = [[0] * 32 for _ in range(4)]
    for ft in range(4):
        for e in range(8):
            for s in range(8):
                mm_e[ft][e][s] = sP.inc()
        for i in range(32):
            tr_done[ft][i] = sP.inc()

    def s_all(s):
        for d in range(DC):
            if d >= 2:
                s.wait_ge(sE.h, cast_done[d - 2])
            s.dma_start(out=xftmp[:, (d % 2) * 1024:(d % 2 + 1) * 1024],
                        in_=xst[d * 128:(d + 1) * 128, :]).then_inc(xld[d][0].h, 16)
        for d in range(DC):
            s.dma_start(out=rt[:, d * 8:(d + 1) * 8],
                        in_=routerT[d * 128:(d + 1) * 128, :]).then_inc(rtld[d][0].h, 16)
        for ft in range(4):
            for j in range(4):
                fc = ft * 4 + j
                if ft >= 1:
                    s.wait_ge(sV.h, oev[ft - 1][31])
                    if final_out:
                        s.wait_ge(sP.h, tr_done[ft - 1][31])
                s.dma_start(out=xres[:, j * 1024:(j + 1) * 1024],
                            in_=xst[fc * 128:(fc + 1) * 128, :]
                            ).then_inc(xrld[ft][j][0].h, 16)
            if final_out:
                for ss in range(8):
                    s.wait_ge(sV.h, oev[ft][ss * 4 + 3])
                    s.dma_start(out=xout[ss * 128:(ss + 1) * 128,
                                         ft * 512:(ft + 1) * 512],
                                in_=outst[:, ss * 512:(ss + 1) * 512]
                                ).then_inc(st_i[ft][ss][0].h, 16)
            else:
                for j in range(4):
                    fc = ft * 4 + j
                    s.wait_ge(sV.h, oev[ft][7 * 4 + j])
                    s.dma_start(out=xout[fc * 128:(fc + 1) * 128, :],
                                in_=outst[:, j * 1024:(j + 1) * 1024]
                                ).then_inc(st_i[ft][j][0].h, 16)
        for stx in (xs_, rt_s, we, xr, st, si_s):
            stx.barrier(s)
        s.wait_ge(sV.h, sV.n)
    p.add("sync", s_all)

    def e_all(sc):
        for d in range(DC):
            wtok(sc, xld[d])
            sc.copy(out=xlnb[:, d * 1024:(d + 1) * 1024],
                    in_=xftmp[:, (d % 2) * 1024:(d % 2 + 1) * 1024]).then_inc(sE.h, 1)
        for d in range(DC):
            if d >= 2:
                sc.wait_ge(sP.h, stats_done[d - 2])
            sc.activation(out=xsq[:, (d % 2) * 1024:(d % 2 + 1) * 1024],
                          in_=xlnb[:, d * 1024:(d + 1) * 1024],
                          func=AF.Square).then_inc(sE.h, 1)
        sc.wait_ge(sV.h, var_done)
        sc.activation(out=srow[0:1, 2048:3072], in_=srow[0:1, 1024:2048],
                      func=AF.Sqrt, bias=epsc[0:1, 0:1]).then_inc(sE.h, 1)
        for ft in range(4):
            for e in range(8):
                gi = ft * 8 + e
                if gi >= 2:
                    sc.wait_ge(sP.h, mm_e[(gi - 2) // 8][(gi - 2) % 8][7])
                for k in range(16):
                    sc.dma_start(
                        out=wet[:, ((gi % 2) * 16 + k) * 512:((gi % 2) * 16 + k + 1) * 512],
                        in_=experts[e * 2048 + ft * 512:e * 2048 + (ft + 1) * 512,
                                    k * 128:(k + 1) * 128],
                        transpose=True).then_inc(weld[ft][e][0].h, 16)
    p.add("scalar", e_all)

    def v_all(v):
        v.memset(onesb[:], 1.0)
        v.memset(epsc[:], 1e-5)
        v.memset(onesrf[:], 1.0).then_inc(sV.h, 1)
        v.wait_ge(sP.h, stats_done[DC - 1])
        last = None
        for s in range(NS):
            mean = srow[0:1, s * 512:(s + 1) * 512]
            e2 = srow[0:1, 1024 + s * 512:1024 + (s + 1) * 512]
            scr = srow[0:1, 2048 + s * 512:2048 + (s + 1) * 512]
            ins = v.tensor_scalar(out=mean, in0=psS1[s][:], scalar1=1.0 / D,
                                  scalar2=None, op0=ALU.mult)
            zsync(v, ins)
            ins = v.tensor_scalar(out=e2, in0=psS2[s][:], scalar1=1.0 / D,
                                  scalar2=None, op0=ALU.mult)
            zsync(v, ins)
            ins = v.tensor_mul(out=scr, in0=mean, in1=mean)
            zsync(v, ins)
            last = v.tensor_sub(out=e2, in0=e2, in1=scr)
        last.then_inc(sV.h, 1)
        v.wait_ge(sE.h, sd_done)
        v.reciprocal(out=srow[0:1, 1024:2048],
                     in_=srow[0:1, 2048:3072]).then_inc(sV.h, 1)
        v.wait_ge(sP.h, bc_done)
        last = None
        for d in range(DC):
            for s in range(NS):
                sl = slice(d * 1024 + s * 512, d * 1024 + (s + 1) * 512)
                tmp = xftmp[:, s * 512:(s + 1) * 512]
                v.tensor_sub(out=tmp, in0=xlnb[:, sl],
                             in1=mbc[:, s * 512:(s + 1) * 512])
                last = v.tensor_mul(out=xlnb[:, sl], in0=tmp,
                                    in1=rbc[:, s * 512:(s + 1) * 512])
        last.then_inc(sV.h, 1)
        v.wait_ge(sP.h, lg_done)
        v.tensor_copy(out=lgs[:], in_=lps[:]).then_inc(sV.h, 1)
        for ss in range(8):
            v.wait_ge(sP.h, ltp_done[ss])
            ins = v.reduce_max(out=masks[:, ss * 9 + 8:ss * 9 + 9], in_=ltp[:],
                               axis=AX.X)
            zsync(v, ins)
            ins = v.tensor_scalar(out=masks[:, ss * 9:ss * 9 + 8], in0=ltp[:],
                                  scalar1=masks[:, ss * 9 + 8:ss * 9 + 9],
                                  scalar2=None, op0=ALU.is_ge)
            zsync(v, ins)
            v.engine_nop().then_inc(sV.h, 1)
        for ft in range(4):
            for e in range(8):
                for s in range(8):
                    a = (ft * 64 + e * 8 + s) % 2
                    v.wait_ge(sP.h, mm_e[ft][e][s])
                    if e == 0:
                        v.tensor_scalar(out=accb[:, s * 512:(s + 1) * 512],
                                        in0=xpp[a][:],
                                        scalar1=masks[:, s * 9 + e:s * 9 + e + 1],
                                        scalar2=None,
                                        op0=ALU.mult).then_inc(sV.h, 1)
                    else:
                        v.scalar_tensor_tensor(out=accb[:, s * 512:(s + 1) * 512],
                                               in0=xpp[a][:],
                                               scalar=masks[:, s * 9 + e:s * 9 + e + 1],
                                               in1=accb[:, s * 512:(s + 1) * 512],
                                               op0=ALU.mult,
                                               op1=ALU.add).then_inc(sV.h, 1)
            for s in range(8):
                for j in range(4):
                    i = s * 4 + j
                    a = i % 2
                    v.wait_ge(sP.h, tr_done[ft][i])
                    wtok(v, xrld[ft][j])
                    if ft >= 1:
                        wtok(v, st_i[ft - 1][7 if final_out else 3])
                    if final_out:
                        v.scalar_tensor_tensor(
                            out=outst[:, s * 512 + j * 128:s * 512 + (j + 1) * 128],
                            in0=accb[:, s * 512 + j * 128:s * 512 + (j + 1) * 128],
                            scalar=coef, in1=ttp[a][:],
                            op0=ALU.mult, op1=ALU.add).then_inc(sV.h, 1)
                    else:
                        v.scalar_tensor_tensor(
                            out=outst[:, j * 1024 + s * 128:j * 1024 + (s + 1) * 128],
                            in0=ttp[a][:], scalar=coef,
                            in1=xres[:, j * 1024 + s * 128:j * 1024 + (s + 1) * 128],
                            op0=ALU.mult, op1=ALU.add).then_inc(sV.h, 1)
    p.add("vector", v_all)

    def t_all(t):
        t.wait_ge(sV.h, init_done)
        for d in range(DC):
            t.wait_ge(sE.h, sq_done[d])
            for s in range(NS):
                sl = slice(d * 1024 + s * 512, d * 1024 + (s + 1) * 512)
                sq = slice((d % 2) * 1024 + s * 512, (d % 2) * 1024 + (s + 1) * 512)
                t.matmul(psS1[s][:], lhsT=onesb[:], rhs=xlnb[:, sl],
                         start=(d == 0), stop=(d == DC - 1))
                t.matmul(psS2[s][:], lhsT=onesb[:], rhs=xsq[:, sq],
                         start=(d == 0), stop=(d == DC - 1)).then_inc(sP.h, 2)
        t.wait_ge(sV.h, rstd_done)
        for s in range(NS):
            t.matmul(mbc[:, s * 512:(s + 1) * 512], lhsT=onesrf[:],
                     rhs=srow[0:1, s * 512:(s + 1) * 512], start=True, stop=True)
            last = t.matmul(rbc[:, s * 512:(s + 1) * 512], lhsT=onesrf[:],
                            rhs=srow[0:1, 1024 + s * 512:1024 + (s + 1) * 512],
                            start=True, stop=True)
        last.then_inc(sP.h, 1)
        t.wait_ge(sV.h, xln_done)
        wtok(t, rt_all)
        for s in range(NS):
            for d in range(DC):
                last = t.matmul(lps[0:8, s * 512:(s + 1) * 512],
                                lhsT=rt[:, d * 8:(d + 1) * 8],
                                rhs=xlnb[:, d * 1024 + s * 512:d * 1024 + (s + 1) * 512],
                                start=(d == 0), stop=(d == DC - 1))
        last.then_inc(sP.h, 1)
        wtok(t, idld)
        t.wait_ge(sV.h, lgs_done)
        for ss in range(8):
            if ss >= 1:
                t.wait_ge(sV.h, msk_done[ss - 1])
            t.transpose(out=ltp[:], in_=lgs[0:8, ss * 128:(ss + 1) * 128],
                        identity=idt[0:8, 0:8]).then_inc(sP.h, 1)
        for ft in range(4):
            for e in range(8):
                wtok(t, weld[ft][e])
                gi = ft * 8 + e
                pq = gi % 2
                for s in range(8):
                    a = (ft * 64 + e * 8 + s) % 2
                    if ft * 64 + e * 8 + s >= 2:
                        pi = ft * 64 + e * 8 + s - 2
                        t.wait_ge(sV.h, ev[pi // 64][(pi % 64) // 8][pi % 8])
                    last = None
                    for k in range(16):
                        last = t.matmul(
                            xpp[a][:],
                            lhsT=xlnb[:, k * 1024 + s * 128:k * 1024 + (s + 1) * 128],
                            rhs=wet[:, ((pq * 16) + k) * 512:((pq * 16) + k + 1) * 512],
                            start=(k == 0), stop=(k == 15))
                    last.then_inc(sP.h, 1)
            for s in range(8):
                for j in range(4):
                    i = s * 4 + j
                    a = i % 2
                    if ft * 32 + i >= 2:
                        pi = ft * 32 + i - 2
                        t.wait_ge(sV.h, oev[pi // 32][pi % 32])
                    t.wait_ge(sV.h, ev[ft][7][s])
                    if final_out:
                        wtok(t, xrld[ft][j])
                        t.transpose(out=ttp[a][:],
                                    in_=xres[:, j * 1024 + s * 128:j * 1024 + (s + 1) * 128],
                                    identity=idt[:]).then_inc(sP.h, 1)
                    else:
                        t.transpose(out=ttp[a][:],
                                    in_=accb[:, s * 512 + j * 128:s * 512 + (j + 1) * 128],
                                    identity=idt[:]).then_inc(sP.h, 1)
    p.add("tensor", t_all)

    def g_all(g):
        g.dma_start(out=idt[:], in_=identf).then_inc(idld[0].h, 16)
    p.add("gpsimd", g_all)

    p.run()
    return nc



def build_m1():
    nc = _bass()
    xst = nc.dram_tensor("xst", [D, T], F32, kind="ExternalInput").ap()
    winq = nc.dram_tensor("winq", [8352, D], BF16, kind="ExternalInput").ap()
    zps = nc.dram_tensor("zps", [4096, T], BF16, kind="ExternalOutput").ap()
    xbc = nc.dram_tensor("xbc", [4224, T], BF16, kind="ExternalOutput").ap()
    dtr = nc.dram_tensor("dtr", [32, T], F32, kind="ExternalOutput").ap()

    p = Prog(nc)
    sE = p.sem("sE")
    sV = p.sem("sV")
    sP = p.sem("sP")
    xs_ = Stream(p, "xs", 2)
    wa = Stream(p, "wa", 2)
    ot = Stream(p, "ot", 4)

    xftmp = p.sb("xftmp", [128, 2 * 512], F32)
    xlnb = p.sb("xlnb", [128, DC * 1024], BF16)
    xsq = p.sb("xsq", [128, 2 * 1024], BF16)
    srow = p.sb("srow", [1, 6 * 512], F32)
    onesb = p.sb("onesb", [128, 1], BF16)
    onesrf = p.sb("onesrf", [1, 128], F32)
    epsc = p.sb("epsc", [128, 1], F32)
    wt = p.sb("wt", [128, 2 * 16 * 512], BF16)
    ostg = p.sb("ostg", [128, 4 * 1024], BF16)
    ostf = p.sb("ostf", [128, 1024], F32)

    lnst = contextlib.ExitStack()
    psS1 = [p.ps(f"psS1{s}", [1, 512], F32, stack=lnst) for s in range(NS)]
    psS2 = [p.ps(f"psS2{s}", [1, 512], F32, stack=lnst) for s in range(NS)]
    mbc = p.ps("mbc", [128, 1024], F32, stack=lnst)
    rbc = p.ps("rbc", [128, 1024], F32, stack=lnst)
    lnst.close()
    xp = [p.ps(f"xp{a}", [128, 1024], F32) for a in range(2)]

    MT = 66
    NPN = 17

    def prows(mp):
        return 512 if mp < 16 else 160

    def pchunks(mp):
        return 4 if mp < 16 else 2

    xld = [xs_.tok() for _ in range(DC)]
    xld_all = [(s_, s_.n) for s_ in xs_.sems]
    wld = [wa.tok(16) for _ in range(NPN)]
    otok = [ot.tok() for _ in range(MT)]
    sq_done = [sE.inc(2) for _ in range(DC)]
    sd_done = sE.inc()
    init_done = sV.inc()
    var_done = sV.inc()
    rstd_done = sV.inc()
    xln_done = sV.inc()
    evd = [sV.inc() for _ in range(MT)]
    stats_done = [sP.inc(4) for _ in range(DC)]
    bc_done = sP.inc()
    mmx = [sP.inc() for _ in range(MT)]
    pe_panel = [mmx[min(sum(pchunks(q) for q in range(mp + 1)), MT) - 1]
                for mp in range(NPN)]

    def s_all(s):
        for m in range(MT):
            s.wait_ge(sV.h, evd[m])
            if m < 32:
                dst = zps[m * 128:(m + 1) * 128, :]
            elif m < 65:
                dst = xbc[(m - 32) * 128:(m - 31) * 128, :]
            else:
                dst = dtr
            if m == 65:
                s.dma_start(out=dst, in_=ostf[0:32, :]).then_inc(otok[m][0].h, 16)
            else:
                s.dma_start(out=dst, in_=ostg[:, (m % 4) * 1024:(m % 4 + 1) * 1024]
                            ).then_inc(otok[m][0].h, 16)
        for stx in (xs_, wa, ot):
            stx.barrier(s)
        s.wait_ge(sV.h, sV.n)
    p.add("sync", s_all)

    def g_all(g):
        for d in range(DC):
            g.dma_start(out=xlnb[:, d * 1024:(d + 1) * 1024],
                        in_=xst[d * 128:(d + 1) * 128, :]).then_inc(xld[d][0].h, 16)
    p.add("gpsimd", g_all)

    def e_all(sc):
        sc.wait_ge(sV.h, init_done)
        for tk in xld_all:
            wtok(sc, tk)
        for d in range(DC):
            if d >= 2:
                sc.wait_ge(sP.h, stats_done[d - 2])
            sc.activation(out=xsq[:, (d % 2) * 1024:(d % 2 + 1) * 1024],
                          in_=xlnb[:, d * 1024:(d + 1) * 1024],
                          func=AF.Square).then_inc(sE.h, 2)
        sc.wait_ge(sV.h, var_done)
        sc.activation(out=srow[0:1, 2048:3072], in_=srow[0:1, 1024:2048],
                      func=AF.Sqrt, bias=epsc[0:1, 0:1]).then_inc(sE.h, 1)
        for mp in range(NPN):
            if mp >= 2:
                sc.wait_ge(sP.h, pe_panel[mp - 2])
            rows = prows(mp)
            for k in range(16):
                sc.dma_start(
                    out=wt[:, ((mp % 2) * 16 + k) * 512:((mp % 2) * 16 + k) * 512 + rows],
                    in_=winq[mp * 512:mp * 512 + rows, k * 128:(k + 1) * 128],
                    transpose=True).then_inc(wld[mp][0].h, 16)
    p.add("scalar", e_all)

    def v_all(v):
        v.memset(onesb[:], 1.0)
        v.memset(epsc[:], 1e-5)
        v.memset(onesrf[:], 1.0).then_inc(sV.h, 1)
        v.wait_ge(sP.h, stats_done[DC - 1])
        last = None
        for s in range(NS):
            mean = srow[0:1, s * 512:(s + 1) * 512]
            e2 = srow[0:1, 1024 + s * 512:1024 + (s + 1) * 512]
            scr = srow[0:1, 2048 + s * 512:2048 + (s + 1) * 512]
            v.tensor_scalar(out=mean, in0=psS1[s][:], scalar1=1.0 / D,
                            scalar2=None, op0=ALU.mult)
            v.tensor_scalar(out=e2, in0=psS2[s][:], scalar1=1.0 / D,
                            scalar2=None, op0=ALU.mult)
            v.tensor_mul(out=scr, in0=mean, in1=mean)
            last = v.tensor_sub(out=e2, in0=e2, in1=scr)
        last.then_inc(sV.h, 1)
        v.wait_ge(sE.h, sd_done)
        v.reciprocal(out=srow[0:1, 1024:2048],
                     in_=srow[0:1, 2048:3072]).then_inc(sV.h, 1)
        v.wait_ge(sP.h, bc_done)
        last = None
        for d in range(DC):
            for s in range(NS):
                sl = slice(d * 1024 + s * 512, d * 1024 + (s + 1) * 512)
                tmp = xftmp[:, s * 512:(s + 1) * 512]
                v.tensor_sub(out=tmp, in0=xlnb[:, sl],
                             in1=mbc[:, s * 512:(s + 1) * 512])
                last = v.tensor_mul(out=xlnb[:, sl], in0=tmp,
                                    in1=rbc[:, s * 512:(s + 1) * 512])
        last.then_inc(sV.h, 1)
        for m in range(MT):
            a = m % 2
            nr = 128 if m < 65 else 32
            v.wait_ge(sP.h, mmx[m])
            if m >= 4 and m < 65:
                wtok(v, otok[m - 4])
            if m == 65:
                v.tensor_copy(out=ostf[0:32, :], in_=xp[a][0:32, :]).then_inc(sV.h, 1)
            else:
                v.tensor_copy(out=ostg[:, (m % 4) * 1024:(m % 4 + 1) * 1024],
                              in_=xp[a][:]).then_inc(sV.h, 1)
    p.add("vector", v_all)

    def t_all(t):
        t.wait_ge(sV.h, init_done)
        for d in range(DC):
            t.wait_ge(sE.h, sq_done[d])
            for s in range(NS):
                sl = slice(d * 1024 + s * 512, d * 1024 + (s + 1) * 512)
                sq = slice((d % 2) * 1024 + s * 512, (d % 2) * 1024 + (s + 1) * 512)
                t.matmul(psS1[s][:], lhsT=onesb[:], rhs=xlnb[:, sl],
                         start=(d == 0), stop=(d == DC - 1))
                t.matmul(psS2[s][:], lhsT=onesb[:], rhs=xsq[:, sq],
                         start=(d == 0), stop=(d == DC - 1)).then_inc(sP.h, 2)
        t.wait_ge(sV.h, rstd_done)
        for s in range(NS):
            t.matmul(mbc[:, s * 512:(s + 1) * 512], lhsT=onesrf[:],
                     rhs=srow[0:1, s * 512:(s + 1) * 512], start=True, stop=True)
            last = t.matmul(rbc[:, s * 512:(s + 1) * 512], lhsT=onesrf[:],
                            rhs=srow[0:1, 1024 + s * 512:1024 + (s + 1) * 512],
                            start=True, stop=True)
        last.then_inc(sP.h, 1)
        t.wait_ge(sV.h, xln_done)
        m = 0
        for mp in range(NPN):
            wtok(t, wld[mp])
            for m4 in range(pchunks(mp)):
                a = m % 2
                nr = 128 if m < 65 else 32
                if m >= 2:
                    t.wait_ge(sV.h, evd[m - 2])
                last = None
                for s in range(NS):
                    for k in range(16):
                        last = t.matmul(
                            xp[a][0:nr, s * 512:(s + 1) * 512],
                            lhsT=wt[:, ((mp % 2) * 16 + k) * 512 + m4 * 128:
                                    ((mp % 2) * 16 + k) * 512 + m4 * 128 + nr],
                            rhs=xlnb[:, k * 1024 + s * 512:k * 1024 + (s + 1) * 512],
                            start=(k == 0), stop=(k == 15))
                last.then_inc(sP.h, 1)
                m += 1
    p.add("tensor", t_all)

    p.run()
    return nc



def build_m3():
    nc = _bass()
    xst = nc.dram_tensor("xst", [D, T], F32, kind="ExternalInput").ap()
    yin = nc.dram_tensor("yin", [4096, T], BF16, kind="ExternalInput").ap()
    zin = nc.dram_tensor("zin", [4096, T], BF16, kind="ExternalInput").ap()
    nweight = nc.dram_tensor("nweight", [128, 32], F32, kind="ExternalInput").ap()
    woutq = nc.dram_tensor("woutq", [D, 4096], BF16, kind="ExternalInput").ap()
    xout = nc.dram_tensor("xout", [D, T], F32, kind="ExternalOutput").ap()

    p = Prog(nc)
    sE = p.sem("sE")
    sV = p.sem("sV")
    sP = p.sem("sP")
    yld = Stream(p, "yld", 2)
    zld = Stream(p, "zld", 2)
    wcs = Stream(p, "wcs", 2)
    xr = Stream(p, "xr", 4)
    ot = Stream(p, "ot", 4)
    nw_s = Stream(p, "nw", 1)

    KC = 32
    ybuf = p.sb("ybuf", [128, 2 * 1024], BF16)
    zbuf = p.sb("zbuf", [128, 2 * 1024], BF16)
    szb = p.sb("szb", [128, 2 * 1024], BF16)
    ygb = p.sb("ygb", [128, KC * 1024], BF16)
    ysq = p.sb("ysq", [128, 2 * 1024], BF16)
    srow = p.sb("srow", [1, 6 * 512], F32)
    onesb = p.sb("onesb", [128, 1], BF16)
    onesrf = p.sb("onesrf", [1, 128], F32)
    epsc = p.sb("epsc", [128, 1], F32)
    nwc = p.sb("nwc", [128, KC], F32)
    w3t = p.sb("w3t", [128, 2 * 512], BF16)
    xres = p.sb("xres", [128, 4 * 512], F32)
    xot = p.sb("xot", [128, 4 * 512], F32)

    psS2 = [p.ps(f"psS2{s}", [1, 512], F32) for s in range(NS)]
    rbc2 = [p.ps(f"rbc{s}", [128, 512], F32) for s in range(NS)]
    yp = [p.ps(f"yp{i}", [128, 512], F32) for i in range(4)]

    NRES = DC * NS
    resorder = [(dp * 2 + d4, si) for dp in range(8) for d4 in range(2)
                for si in range(NS)]

    ytok = [yld.tok() for _ in range(KC)]
    ztok = [zld.tok() for _ in range(KC)]
    nwt = nw_s.tok()
    w3tok = [[wcs.tok() for _ in range(KC)] for _ in range(8)]
    xrt = [xr.tok() for _ in range(4)]
    sti = []
    for i in range(NRES):
        sti.append(ot.tok())
        if i + 4 < NRES:
            xrt.append(xr.tok())
    sil_done = [sE.inc() for _ in range(KC)]
    sq_done = [sE.inc() for _ in range(KC)]
    sd_done = sE.inc()
    init_done = sV.inc()
    yg_done = [sV.inc() for _ in range(KC)]
    var_done = sV.inc()
    rstd_done = sV.inc()
    yn_done = [sV.inc() for _ in range(KC)]
    yev = [sV.inc() for _ in range(NRES)]
    stats_done = [sP.inc(2) for _ in range(KC)]
    bc_done = sP.inc()
    w3pe = [[0] * KC for _ in range(8)]
    ype = [0] * 8
    w3pe_flat = []
    for dp in range(8):
        for k in range(KC):
            w3pe[dp][k] = sP.inc()
            w3pe_flat.append(w3pe[dp][k])
        ype[dp] = w3pe[dp][KC - 1]

    def s_all(s):
        s.dma_start(out=nwc[:], in_=nweight).then_inc(nwt[0].h, 16)
        for i in range(4):
            d, si = resorder[i]
            s.dma_start(out=xres[:, (i % 4) * 512:(i % 4 + 1) * 512],
                        in_=xst[d * 128:(d + 1) * 128, si * 512:(si + 1) * 512]
                        ).then_inc(xrt[i][0].h, 16)
        for i in range(NRES):
            d, si = resorder[i]
            s.wait_ge(sV.h, yev[i])
            s.dma_start(out=xout[d * 128:(d + 1) * 128, si * 512:(si + 1) * 512],
                        in_=xot[:, (i % 4) * 512:(i % 4 + 1) * 512]
                        ).then_inc(sti[i][0].h, 16)
            if i + 4 < NRES:
                d2, s2 = resorder[i + 4]
                s.dma_start(out=xres[:, ((i + 4) % 4) * 512:((i + 4) % 4 + 1) * 512],
                            in_=xst[d2 * 128:(d2 + 1) * 128, s2 * 512:(s2 + 1) * 512]
                            ).then_inc(xrt[i + 4][0].h, 16)
        for stx in (yld, zld, wcs, xr, ot, nw_s):
            stx.barrier(s)
        s.wait_ge(sV.h, sV.n)
    p.add("sync", s_all)

    def g_all(g):
        for k in range(KC):
            if k >= 2:
                g.wait_ge(sV.h, yg_done[k - 2])
            g.dma_start(out=ybuf[:, (k % 2) * 1024:(k % 2 + 1) * 1024],
                        in_=yin[k * 128:(k + 1) * 128, :]).then_inc(ytok[k][0].h, 16)
            g.dma_start(out=zbuf[:, (k % 2) * 1024:(k % 2 + 1) * 1024],
                        in_=zin[k * 128:(k + 1) * 128, :]).then_inc(ztok[k][0].h, 16)
    p.add("gpsimd", g_all)

    def e_all(sc):
        for k in range(KC):
            wtok(sc, ztok[k])
            if k >= 2:
                sc.wait_ge(sV.h, yg_done[k - 2])
            sc.activation(out=szb[:, (k % 2) * 1024:(k % 2 + 1) * 1024],
                          in_=zbuf[:, (k % 2) * 1024:(k % 2 + 1) * 1024],
                          func=AF.Silu).then_inc(sE.h, 1)
        for k in range(KC):
            sc.wait_ge(sV.h, yg_done[k])
            if k >= 2:
                sc.wait_ge(sP.h, stats_done[k - 2])
            sc.activation(out=ysq[:, (k % 2) * 1024:(k % 2 + 1) * 1024],
                          in_=ygb[:, k * 1024:(k + 1) * 1024],
                          func=AF.Square).then_inc(sE.h, 1)
        sc.wait_ge(sV.h, var_done)
        sc.activation(out=srow[0:1, 2048:3072], in_=srow[0:1, 1024:2048],
                      func=AF.Sqrt, bias=epsc[0:1, 0:1]).then_inc(sE.h, 1)
        for dp in range(8):
            for k in range(KC):
                gi = dp * KC + k
                if gi >= 2:
                    sc.wait_ge(sP.h, w3pe_flat[gi - 2])
                sc.dma_start(out=w3t[:, (k % 2) * 512:(k % 2) * 512 + 256],
                             in_=woutq[dp * 256:(dp + 1) * 256,
                                       k * 128:(k + 1) * 128],
                             transpose=True).then_inc(w3tok[dp][k][0].h, 16)
    p.add("scalar", e_all)

    def v_all(v):
        v.memset(onesb[:], 1.0)
        v.memset(epsc[:], 1e-5)
        v.memset(onesrf[:], 1.0).then_inc(sV.h, 1)
        for k in range(KC):
            wtok(v, ytok[k])
            v.wait_ge(sE.h, sil_done[k])
            v.tensor_mul(out=ygb[:, k * 1024:(k + 1) * 1024],
                         in0=ybuf[:, (k % 2) * 1024:(k % 2 + 1) * 1024],
                         in1=szb[:, (k % 2) * 1024:(k % 2 + 1) * 1024]
                         ).then_inc(sV.h, 1)
        v.wait_ge(sP.h, stats_done[KC - 1])
        last = None
        for s in range(NS):
            e2 = srow[0:1, 1024 + s * 512:1024 + (s + 1) * 512]
            last = v.tensor_scalar(out=e2, in0=psS2[s][:], scalar1=1.0 / 4096.0,
                                   scalar2=None, op0=ALU.mult)
        last.then_inc(sV.h, 1)
        v.wait_ge(sE.h, sd_done)
        v.reciprocal(out=srow[0:1, 1024:2048],
                     in_=srow[0:1, 2048:3072]).then_inc(sV.h, 1)
        wtok(v, nwt)
        v.wait_ge(sP.h, bc_done)
        for k in range(KC):
            last = None
            for s in range(NS):
                sl = slice(k * 1024 + s * 512, k * 1024 + (s + 1) * 512)
                last = v.scalar_tensor_tensor(
                    out=ygb[:, sl], in0=ygb[:, sl], scalar=nwc[:, k:k + 1],
                    in1=rbc2[s][:], op0=ALU.mult, op1=ALU.mult)
            last.then_inc(sV.h, 1)
        for i, (d, si) in enumerate(resorder):
            dp, d4 = d // 2, d % 2
            v.wait_ge(sP.h, ype[dp])
            wtok(v, xrt[i])
            if i >= 4:
                wtok(v, sti[i - 4])
            v.scalar_tensor_tensor(out=xot[:, (i % 4) * 512:(i % 4 + 1) * 512],
                                   in0=yp[d4 * 2 + si][:], scalar=1.0,
                                   in1=xres[:, (i % 4) * 512:(i % 4 + 1) * 512],
                                   op0=ALU.mult, op1=ALU.add).then_inc(sV.h, 1)
    p.add("vector", v_all)

    def t_all(t):
        t.wait_ge(sV.h, init_done)
        for k in range(KC):
            t.wait_ge(sE.h, sq_done[k])
            for s in range(NS):
                sq = slice((k % 2) * 1024 + s * 512, (k % 2) * 1024 + (s + 1) * 512)
                t.matmul(psS2[s][:], lhsT=onesb[:], rhs=ysq[:, sq],
                         start=(k == 0), stop=(k == KC - 1)).then_inc(sP.h, 1)
        t.wait_ge(sV.h, rstd_done)
        last = None
        for s in range(NS):
            last = t.matmul(rbc2[s][:], lhsT=onesrf[:],
                            rhs=srow[0:1, 1024 + s * 512:1024 + (s + 1) * 512],
                            start=True, stop=True)
        last.then_inc(sP.h, 1)
        for dp in range(8):
            for k in range(KC):
                wtok(t, w3tok[dp][k])
                t.wait_ge(sV.h, yn_done[k])
                if k == 0 and dp >= 1:
                    for i4 in range(4):
                        t.wait_ge(sV.h, yev[(dp - 1) * 4 + i4])
                last = None
                for d4 in range(2):
                    for s in range(NS):
                        last = t.matmul(
                            yp[d4 * 2 + s][:],
                            lhsT=w3t[:, (k % 2) * 512 + d4 * 128:
                                     (k % 2) * 512 + (d4 + 1) * 128],
                            rhs=ygb[:, k * 1024 + s * 512:k * 1024 + (s + 1) * 512],
                            start=(k == 0), stop=(k == KC - 1))
                last.then_inc(sP.h, 1)
    p.add("tensor", t_all)

    p.run()
    return nc



def build_mla1():
    nc = _bass()
    xst = nc.dram_tensor("xst", [D, T], F32, kind="ExternalInput").ap()
    cqw = nc.dram_tensor("cqw", [D, 768], BF16, kind="ExternalInput").ap()
    ckvw = nc.dram_tensor("ckvw", [D, 256], BF16, kind="ExternalInput").ap()
    krw = nc.dram_tensor("krw", [D, 32], BF16, kind="ExternalInput").ap()
    qg = nc.dram_tensor("qg", [128, 6], F32, kind="ExternalInput").ap()
    kvg = nc.dram_tensor("kvg", [128, 2], F32, kind="ExternalInput").ap()
    costk = nc.dram_tensor("costk", [128, 8 * 16], F32, kind="ExternalInput").ap()
    identb = nc.dram_tensor("identb", [128, 128], BF16, kind="ExternalInput").ap()
    nq = nc.dram_tensor("nq", [768, T], BF16, kind="ExternalOutput").ap()
    nkv = nc.dram_tensor("nkv", [256, T], BF16, kind="ExternalOutput").ap()
    krr = nc.dram_tensor("krr", [32, T], BF16, kind="ExternalOutput").ap()

    p = Prog(nc)
    sE = p.sem("sE")
    sV = p.sem("sV")
    sP = p.sem("sP")
    xs_ = Stream(p, "xs", 1)
    wl = Stream(p, "wl", 1)
    ot = Stream(p, "ot", 4)

    xftmp = p.sb("xftmp", [128, 2 * 512], F32)
    xlnb = p.sb("xlnb", [128, DC * 1024], BF16)
    xsq = p.sb("xsq", [128, 2 * 1024], BF16)
    srow = p.sb("srow", [1, 6 * 512], F32)
    onesb = p.sb("onesb", [128, 1], BF16)
    onesrf = p.sb("onesrf", [1, 128], F32)
    epsc = p.sb("epsc", [128, 1], F32)
    eps6 = p.sb("eps6", [128, 1], F32)
    cqs = p.sb("cqs", [128, 6 * 1024], BF16)
    cqsq = p.sb("cqsq", [128, 6 * 1024], BF16)
    wcq = p.sb("wcq", [128, DC * 768], BF16)
    wckv = p.sb("wckv", [128, DC * 256], BF16)
    wkr = p.sb("wkr", [128, DC * 32], BF16)
    qgc = p.sb("qgc", [128, 6], F32)
    kvgc = p.sb("kvgc", [128, 2], F32)
    ctk = p.sb("ctk", [128, 8 * 16], F32)
    idb = p.sb("idb", [128, 128], BF16)
    ost = p.sb("ost", [128, 4 * 1024], BF16)
    krt = p.sb("krt", [128, 2 * 32], F32)
    krro = p.sb("krro", [128, 8 * 32], BF16)
    krst = p.sb("krst", [32, 1024], BF16)

    lnst = contextlib.ExitStack()
    psS1 = [p.ps(f"psS1{s}", [1, 512], F32, stack=lnst) for s in range(NS)]
    psS2 = [p.ps(f"psS2{s}", [1, 512], F32, stack=lnst) for s in range(NS)]
    mbc = p.ps("mbc", [128, 1024], F32, stack=lnst)
    rbc = p.ps("rbc", [128, 1024], F32, stack=lnst)
    lnst.close()
    cqp = [p.ps(f"cqp{a}", [128, 512], F32) for a in range(2)]
    ps2b = [p.ps(f"ps2b{s}", [1, 512], F32) for s in range(NS)]
    rbq = [p.ps(f"rbq{s}", [128, 512], F32) for s in range(NS)]
    krp = p.ps("krp", [128, 32], F32)
    krtp = p.ps("krtp", [32, 128], BF16)

    xld = [xs_.tok() for _ in range(DC)]
    xld_all = (xs_.sems[0], 16 * DC)
    for _ in range(DC * 3 + 4):
        wl.tok()
    wl_all = (wl.sems[0], wl.sems[0].n)
    otok = [ot.tok() for _ in range(9)]
    sq_done = [sE.inc() for _ in range(DC)]
    sd_done = sE.inc()
    q_sq = [sE.inc() for _ in range(6)]
    q_sd = sE.inc()
    kv_sq = [sE.inc() for _ in range(2)]
    kv_sd = sE.inc()
    init_done = sV.inc()
    var_done = sV.inc()
    rstd_done = sV.inc()
    xln_done = sV.inc()
    cq_ev = [sV.inc() for _ in range(6)]
    q_var = sV.inc()
    q_rstd = sV.inc()
    nq_ev = [sV.inc() for _ in range(6)]
    ckv_ev = [sV.inc() for _ in range(2)]
    kv_var = sV.inc()
    kv_rstd = sV.inc()
    nkv_ev = [sV.inc() for _ in range(2)]
    kr_cp = [sV.inc(3) for _ in range(8)]
    kr_tp = [sV.inc() for _ in range(8)]
    stats_done = [sP.inc(4) for _ in range(DC)]
    bc_done = sP.inc()
    cq_mm = [sP.inc() for _ in range(6)]
    q_st = sP.inc()
    q_bc = sP.inc()
    ckv_mm = [sP.inc() for _ in range(2)]
    kv_st = sP.inc()
    kv_bc = sP.inc()
    kr_mm = [sP.inc() for _ in range(8)]
    kr_pe = [sP.inc() for _ in range(8)]

    def g_all(g):
        for d in range(DC):
            g.dma_start(out=xlnb[:, d * 1024:(d + 1) * 1024],
                        in_=xst[d * 128:(d + 1) * 128, :]).then_inc(xld[d][0].h, 16)
        for d in range(DC):
            g.dma_start(out=wcq[:, d * 768:(d + 1) * 768],
                        in_=cqw[d * 128:(d + 1) * 128, :]).then_inc(wl.sems[0].h, 16)
        for d in range(DC):
            g.dma_start(out=wckv[:, d * 256:(d + 1) * 256],
                        in_=ckvw[d * 128:(d + 1) * 128, :]).then_inc(wl.sems[0].h, 16)
        for d in range(DC):
            g.dma_start(out=wkr[:, d * 32:(d + 1) * 32],
                        in_=krw[d * 128:(d + 1) * 128, :]).then_inc(wl.sems[0].h, 16)
        g.dma_start(out=qgc[:], in_=qg).then_inc(wl.sems[0].h, 16)
        g.dma_start(out=kvgc[:], in_=kvg).then_inc(wl.sems[0].h, 16)
        g.dma_start(out=ctk[:], in_=costk).then_inc(wl.sems[0].h, 16)
        g.dma_start(out=idb[:], in_=identb).then_inc(wl.sems[0].h, 16)
    p.add("gpsimd", g_all)

    def s_all(s):
        oi = 0
        for m in range(6):
            s.wait_ge(sV.h, nq_ev[m])
            s.dma_start(out=nq[m * 128:(m + 1) * 128, :],
                        in_=ost[:, (oi % 4) * 1024:(oi % 4 + 1) * 1024]
                        ).then_inc(otok[oi][0].h, 16)
            oi += 1
        for m in range(2):
            s.wait_ge(sV.h, nkv_ev[m])
            s.dma_start(out=nkv[m * 128:(m + 1) * 128, :],
                        in_=ost[:, (oi % 4) * 1024:(oi % 4 + 1) * 1024]
                        ).then_inc(otok[oi][0].h, 16)
            oi += 1
        s.wait_ge(sV.h, kr_tp[7])
        s.dma_start(out=krr, in_=krst[:]).then_inc(otok[oi][0].h, 16)
        for stx in (xs_, wl, ot):
            stx.barrier(s)
        s.wait_ge(sV.h, sV.n)
        s.wait_ge(sP.h, sP.n)
    p.add("sync", s_all)

    def e_all(sc):
        sc.wait_ge(sV.h, init_done)
        wtok(sc, xld_all)
        for d in range(DC):
            if d >= 2:
                sc.wait_ge(sP.h, stats_done[d - 2])
            sc.activation(out=xsq[:, (d % 2) * 1024:(d % 2 + 1) * 1024],
                          in_=xlnb[:, d * 1024:(d + 1) * 1024],
                          func=AF.Square).then_inc(sE.h, 1)
        sc.wait_ge(sV.h, var_done)
        sc.activation(out=srow[0:1, 2048:3072], in_=srow[0:1, 1024:2048],
                      func=AF.Sqrt, bias=epsc[0:1, 0:1]).then_inc(sE.h, 1)
        for m in range(6):
            sc.wait_ge(sV.h, cq_ev[m])
            sc.activation(out=cqsq[:, m * 1024:(m + 1) * 1024],
                          in_=cqs[:, m * 1024:(m + 1) * 1024],
                          func=AF.Square).then_inc(sE.h, 1)
        sc.wait_ge(sV.h, q_var)
        sc.activation(out=srow[0:1, 2048:3072], in_=srow[0:1, 1024:2048],
                      func=AF.Sqrt, bias=eps6[0:1, 0:1]).then_inc(sE.h, 1)
        for m in range(2):
            sc.wait_ge(sV.h, ckv_ev[m])
            sc.activation(out=cqsq[:, m * 1024:(m + 1) * 1024],
                          in_=cqs[:, m * 1024:(m + 1) * 1024],
                          func=AF.Square).then_inc(sE.h, 1)
        sc.wait_ge(sV.h, kv_var)
        sc.activation(out=srow[0:1, 2048:3072], in_=srow[0:1, 1024:2048],
                      func=AF.Sqrt, bias=eps6[0:1, 0:1]).then_inc(sE.h, 1)
    p.add("scalar", e_all)

    def v_all(v):
        v.memset(onesb[:], 1.0)
        v.memset(epsc[:], 1e-5)
        v.memset(eps6[:], 1e-6)
        v.memset(onesrf[:], 1.0).then_inc(sV.h, 1)
        v.wait_ge(sP.h, stats_done[DC - 1])
        last = None
        for s in range(NS):
            mean = srow[0:1, s * 512:(s + 1) * 512]
            e2 = srow[0:1, 1024 + s * 512:1024 + (s + 1) * 512]
            scr = srow[0:1, 2048 + s * 512:2048 + (s + 1) * 512]
            v.tensor_scalar(out=mean, in0=psS1[s][:], scalar1=1.0 / D,
                            scalar2=None, op0=ALU.mult)
            v.tensor_scalar(out=e2, in0=psS2[s][:], scalar1=1.0 / D,
                            scalar2=None, op0=ALU.mult)
            v.tensor_mul(out=scr, in0=mean, in1=mean)
            last = v.tensor_sub(out=e2, in0=e2, in1=scr)
        last.then_inc(sV.h, 1)
        v.wait_ge(sE.h, sd_done)
        v.reciprocal(out=srow[0:1, 1024:2048],
                     in_=srow[0:1, 2048:3072]).then_inc(sV.h, 1)
        v.wait_ge(sP.h, bc_done)
        last = None
        for d in range(DC):
            for s in range(NS):
                sl = slice(d * 1024 + s * 512, d * 1024 + (s + 1) * 512)
                tmp = xftmp[:, s * 512:(s + 1) * 512]
                v.tensor_sub(out=tmp, in0=xlnb[:, sl],
                             in1=mbc[:, s * 512:(s + 1) * 512])
                last = v.tensor_mul(out=xlnb[:, sl], in0=tmp,
                                    in1=rbc[:, s * 512:(s + 1) * 512])
        last.then_inc(sV.h, 1)
        for m in range(6):
            for s in range(NS):
                v.wait_ge(sP.h, cq_mm[m])
                last = v.tensor_copy(out=cqs[:, m * 1024 + s * 512:m * 1024 + (s + 1) * 512],
                                     in_=cqp[s][:])
            last.then_inc(sV.h, 1)
        v.wait_ge(sP.h, q_st)
        last = None
        for s in range(NS):
            e2 = srow[0:1, 1024 + s * 512:1024 + (s + 1) * 512]
            last = v.tensor_scalar(out=e2, in0=ps2b[s][:], scalar1=1.0 / 768.0,
                                   scalar2=None, op0=ALU.mult)
        last.then_inc(sV.h, 1)
        v.wait_ge(sE.h, q_sd)
        v.reciprocal(out=srow[0:1, 1024:2048],
                     in_=srow[0:1, 2048:3072]).then_inc(sV.h, 1)
        v.wait_ge(sP.h, q_bc)
        for m in range(6):
            if m >= 4:
                wtok(v, otok[m - 4])
            last = None
            for s in range(NS):
                sl = slice(m * 1024 + s * 512, m * 1024 + (s + 1) * 512)
                last = v.scalar_tensor_tensor(
                    out=ost[:, (m % 4) * 1024 + s * 512:(m % 4) * 1024 + (s + 1) * 512],
                    in0=cqs[:, sl], scalar=qgc[:, m:m + 1], in1=rbq[s][:],
                    op0=ALU.mult, op1=ALU.mult)
            last.then_inc(sV.h, 1)
        for m in range(2):
            for s in range(NS):
                v.wait_ge(sP.h, ckv_mm[m])
                last = v.tensor_copy(out=cqs[:, m * 1024 + s * 512:m * 1024 + (s + 1) * 512],
                                     in_=cqp[s][:])
            last.then_inc(sV.h, 1)
        v.wait_ge(sP.h, kv_st)
        last = None
        for s in range(NS):
            e2 = srow[0:1, 1024 + s * 512:1024 + (s + 1) * 512]
            last = v.tensor_scalar(out=e2, in0=ps2b[s][:], scalar1=1.0 / 256.0,
                                   scalar2=None, op0=ALU.mult)
        last.then_inc(sV.h, 1)
        v.wait_ge(sE.h, kv_sd)
        v.reciprocal(out=srow[0:1, 1024:2048],
                     in_=srow[0:1, 2048:3072]).then_inc(sV.h, 1)
        v.wait_ge(sP.h, kv_bc)
        for m in range(2):
            oi = 6 + m
            if oi >= 4:
                wtok(v, otok[oi - 4])
            last = None
            for s in range(NS):
                sl = slice(m * 1024 + s * 512, m * 1024 + (s + 1) * 512)
                last = v.scalar_tensor_tensor(
                    out=ost[:, (oi % 4) * 1024 + s * 512:(oi % 4) * 1024 + (s + 1) * 512],
                    in0=cqs[:, sl], scalar=kvgc[:, m:m + 1], in1=rbq[s][:],
                    op0=ALU.mult, op1=ALU.mult)
            last.then_inc(sV.h, 1)
        for ss in range(8):
            a = ss % 2
            v.wait_ge(sP.h, kr_mm[ss])
            v.tensor_copy(out=krt[:, a * 32:(a + 1) * 32],
                          in_=krp[:]).then_inc(sV.h, 1)
            v.wait_ge(sV.h, kr_cp[ss] - 2)
            e = krt[:, a * 32:(a + 1) * 32:2]
            o = krt[:, a * 32 + 1:(a + 1) * 32:2]
            c = ctk[:, ss * 16:(ss + 1) * 16]
            t1 = xftmp[:, 0:16]
            t2 = xftmp[:, 16:32]
            v.tensor_sub(out=t1, in0=e, in1=o)
            v.tensor_add(out=t2, in0=e, in1=o).then_inc(sV.h, 1)
            v.wait_ge(sV.h, kr_cp[ss] - 1)
            v.tensor_mul(out=krro[:, ss * 32:ss * 32 + 16], in0=t1, in1=c)
            v.tensor_mul(out=krro[:, ss * 32 + 16:ss * 32 + 32], in0=t2, in1=c
                         ).then_inc(sV.h, 1)
        for ss in range(8):
            v.wait_ge(sP.h, kr_pe[ss])
            v.tensor_copy(out=krst[:, ss * 128:(ss + 1) * 128],
                          in_=krtp[:]).then_inc(sV.h, 1)
    p.add("vector", v_all)

    def t_all(t):
        t.wait_ge(sV.h, init_done)
        for d in range(DC):
            t.wait_ge(sE.h, sq_done[d])
            for s in range(NS):
                sl = slice(d * 1024 + s * 512, d * 1024 + (s + 1) * 512)
                sq = slice((d % 2) * 1024 + s * 512, (d % 2) * 1024 + (s + 1) * 512)
                t.matmul(psS1[s][:], lhsT=onesb[:], rhs=xlnb[:, sl],
                         start=(d == 0), stop=(d == DC - 1))
                t.matmul(psS2[s][:], lhsT=onesb[:], rhs=xsq[:, sq],
                         start=(d == 0), stop=(d == DC - 1)).then_inc(sP.h, 2)
        t.wait_ge(sV.h, rstd_done)
        for s in range(NS):
            t.matmul(mbc[:, s * 512:(s + 1) * 512], lhsT=onesrf[:],
                     rhs=srow[0:1, s * 512:(s + 1) * 512], start=True, stop=True)
            last = t.matmul(rbc[:, s * 512:(s + 1) * 512], lhsT=onesrf[:],
                            rhs=srow[0:1, 1024 + s * 512:1024 + (s + 1) * 512],
                            start=True, stop=True)
        last.then_inc(sP.h, 1)
        t.wait_ge(sV.h, xln_done)
        wtok(t, wl_all)
        for m in range(6):
            if m >= 1:
                t.wait_ge(sV.h, cq_ev[m - 1])
            last = None
            for s in range(NS):
                for k in range(DC):
                    last = t.matmul(
                        cqp[s][:],
                        lhsT=wcq[:, k * 768 + m * 128:k * 768 + (m + 1) * 128],
                        rhs=xlnb[:, k * 1024 + s * 512:k * 1024 + (s + 1) * 512],
                        start=(k == 0), stop=(k == DC - 1))
            last.then_inc(sP.h, 1)
        last = None
        for m in range(6):
            t.wait_ge(sE.h, q_sq[m])
            for s in range(NS):
                sq = slice(m * 1024 + s * 512, m * 1024 + (s + 1) * 512)
                last = t.matmul(ps2b[s][:], lhsT=onesb[:], rhs=cqsq[:, sq],
                                start=(m == 0), stop=(m == 5))
        last.then_inc(sP.h, 1)
        t.wait_ge(sV.h, q_rstd)
        last = None
        for s in range(NS):
            last = t.matmul(rbq[s][:], lhsT=onesrf[:],
                            rhs=srow[0:1, 1024 + s * 512:1024 + (s + 1) * 512],
                            start=True, stop=True)
        last.then_inc(sP.h, 1)
        for m in range(2):
            if m == 0:
                t.wait_ge(sV.h, nq_ev[5])
            else:
                t.wait_ge(sV.h, ckv_ev[m - 1])
            last = None
            for s in range(NS):
                for k in range(DC):
                    last = t.matmul(
                        cqp[s][:],
                        lhsT=wckv[:, k * 256 + m * 128:k * 256 + (m + 1) * 128],
                        rhs=xlnb[:, k * 1024 + s * 512:k * 1024 + (s + 1) * 512],
                        start=(k == 0), stop=(k == DC - 1))
            last.then_inc(sP.h, 1)
        last = None
        for m in range(2):
            t.wait_ge(sE.h, kv_sq[m])
            for s in range(NS):
                sq = slice(m * 1024 + s * 512, m * 1024 + (s + 1) * 512)
                last = t.matmul(ps2b[s][:], lhsT=onesb[:], rhs=cqsq[:, sq],
                                start=(m == 0), stop=(m == 1))
        last.then_inc(sP.h, 1)
        t.wait_ge(sV.h, kv_rstd)
        last = None
        for s in range(NS):
            last = t.matmul(rbq[s][:], lhsT=onesrf[:],
                            rhs=srow[0:1, 1024 + s * 512:1024 + (s + 1) * 512],
                            start=True, stop=True)
        last.then_inc(sP.h, 1)
        for ss in range(8):
            if ss >= 1:
                t.wait_ge(sV.h, kr_cp[ss - 1])
            last = None
            for k in range(DC):
                last = t.matmul(krp[:],
                                lhsT=xlnb[:, k * 1024 + ss * 128:k * 1024 + (ss + 1) * 128],
                                rhs=wkr[:, k * 32:(k + 1) * 32],
                                start=(k == 0), stop=(k == DC - 1))
            last.then_inc(sP.h, 1)
        for ss in range(8):
            t.wait_ge(sV.h, kr_cp[ss])
            if ss >= 1:
                t.wait_ge(sV.h, kr_tp[ss - 1])
            t.transpose(out=krtp[:], in_=krro[:, ss * 32:(ss + 1) * 32],
                        identity=idb[:]).then_inc(sP.h, 1)
    p.add("tensor", t_all)

    p.run()
    return nc



def build_mla3():
    nc = _bass()
    xst = nc.dram_tensor("xst", [D, T], F32, kind="ExternalInput").ap()
    outt = nc.dram_tensor("outt", [T, 1024], BF16, kind="ExternalInput").ap()
    projw = nc.dram_tensor("projw", [1024, D], BF16, kind="ExternalInput").ap()
    identb = nc.dram_tensor("identb", [128, 128], BF16, kind="ExternalInput").ap()
    xout = nc.dram_tensor("xout", [D, T], F32, kind="ExternalOutput").ap()

    p = Prog(nc)
    sV = p.sem("sV")
    sP = p.sem("sP")
    ld = Stream(p, "ld", 1)
    xr = Stream(p, "xr", 4)
    ot = Stream(p, "ot", 4)

    otb = p.sb("otb", [128, 8 * 1024], BF16)
    ofm = p.sb("ofm", [128, 8 * 1024], BF16)
    pw = p.sb("pw", [128, 8 * 2048], BF16)
    idb = p.sb("idb", [128, 128], BF16)
    xres = p.sb("xres", [128, 4 * 512], F32)
    xot = p.sb("xot", [128, 4 * 512], F32)

    tpp = [p.ps(f"tpp{a}", [128, 128], BF16) for a in range(2)]
    pj = [p.ps(f"pj{a}", [128, 512], F32) for a in range(2)]

    NRES = DC * NS
    resorder = [(d, si) for d in range(DC) for si in range(NS)]

    for _ in range(8 + 8 + 1):
        ld.tok()
    ld_all = (ld.sems[0], ld.sems[0].n)
    xrt = [xr.tok() for _ in range(4)]
    sti = []
    for i in range(NRES):
        sti.append(ot.tok())
        if i + 4 < NRES:
            xrt.append(xr.tok())
    tp_ev = [sV.inc() for _ in range(64)]
    yev = [sV.inc() for _ in range(NRES)]
    tp_pe = [sP.inc() for _ in range(64)]
    pj_pe = [sP.inc() for _ in range(NRES)]

    def g_all(g):
        for i in range(8):
            g.dma_start(out=otb[:, i * 1024:(i + 1) * 1024],
                        in_=outt[i * 128:(i + 1) * 128, :]).then_inc(ld.sems[0].h, 16)
        for k in range(8):
            g.dma_start(out=pw[:, k * 2048:(k + 1) * 2048],
                        in_=projw[k * 128:(k + 1) * 128, :]).then_inc(ld.sems[0].h, 16)
        g.dma_start(out=idb[:], in_=identb).then_inc(ld.sems[0].h, 16)
    p.add("gpsimd", g_all)

    def s_all(s):
        for i in range(4):
            d, si = resorder[i]
            s.dma_start(out=xres[:, (i % 4) * 512:(i % 4 + 1) * 512],
                        in_=xst[d * 128:(d + 1) * 128, si * 512:(si + 1) * 512]
                        ).then_inc(xrt[i][0].h, 16)
        for i in range(NRES):
            d, si = resorder[i]
            s.wait_ge(sV.h, yev[i])
            s.dma_start(out=xout[d * 128:(d + 1) * 128, si * 512:(si + 1) * 512],
                        in_=xot[:, (i % 4) * 512:(i % 4 + 1) * 512]
                        ).then_inc(sti[i][0].h, 16)
            if i + 4 < NRES:
                d2, s2 = resorder[i + 4]
                s.dma_start(out=xres[:, ((i + 4) % 4) * 512:((i + 4) % 4 + 1) * 512],
                            in_=xst[d2 * 128:(d2 + 1) * 128, s2 * 512:(s2 + 1) * 512]
                            ).then_inc(xrt[i + 4][0].h, 16)
        for stx in (ld, xr, ot):
            stx.barrier(s)
        s.wait_ge(sV.h, sV.n)
    p.add("sync", s_all)

    def v_all(v):
        for i in range(64):
            v.wait_ge(sP.h, tp_pe[i])
            ts_, kc = i // 8, i % 8
            v.tensor_copy(out=ofm[:, kc * 1024 + ts_ * 128:kc * 1024 + (ts_ + 1) * 128],
                          in_=tpp[i % 2][:]).then_inc(sV.h, 1)
        for i, (d, si) in enumerate(resorder):
            v.wait_ge(sP.h, pj_pe[i])
            wtok(v, xrt[i])
            if i >= 4:
                wtok(v, sti[i - 4])
            v.scalar_tensor_tensor(out=xot[:, (i % 4) * 512:(i % 4 + 1) * 512],
                                   in0=pj[i % 2][:], scalar=1.0,
                                   in1=xres[:, (i % 4) * 512:(i % 4 + 1) * 512],
                                   op0=ALU.mult, op1=ALU.add).then_inc(sV.h, 1)
    p.add("vector", v_all)

    def t_all(t):
        wtok(t, ld_all)
        for i in range(64):
            ts_, kc = i // 8, i % 8
            if i >= 2:
                t.wait_ge(sV.h, tp_ev[i - 2])
            t.transpose(out=tpp[i % 2][:],
                        in_=otb[:, ts_ * 1024 + kc * 128:ts_ * 1024 + (kc + 1) * 128],
                        identity=idb[:]).then_inc(sP.h, 1)
        for i, (d, si) in enumerate(resorder):
            if i >= 2:
                t.wait_ge(sV.h, yev[i - 2])
            for kc in range(8):
                t.wait_ge(sV.h, tp_ev[si * 0 + 63] if False else tp_ev[63])
                break
            t.wait_ge(sV.h, tp_ev[63])
            last = None
            for k in range(8):
                last = t.matmul(pj[i % 2][:],
                                lhsT=pw[:, k * 2048 + d * 128:k * 2048 + (d + 1) * 128],
                                rhs=ofm[:, k * 1024 + si * 512:k * 1024 + (si + 1) * 512],
                                start=(k == 0), stop=(k == 7))
            last.then_inc(sP.h, 1)
    p.add("tensor", t_all)

    p.run()
    return nc



def build_m2():
    nc = _bass()
    TT = 2048
    NCH = 8
    HH = 16
    xpart = nc.dram_tensor("xpart", [HH * 128, TT], BF16, kind="ExternalInput").ap()
    bin_ = nc.dram_tensor("bin", [64, TT], BF16, kind="ExternalInput").ap()
    cin_ = nc.dram_tensor("cin", [64, TT], BF16, kind="ExternalInput").ap()
    dtrr = nc.dram_tensor("dtrr", [HH, TT], F32, kind="ExternalInput").ap()
    convwx = nc.dram_tensor("convwx", [128, HH * 4], F32, kind="ExternalInput").ap()
    convwb = nc.dram_tensor("convwb", [64, 8], F32, kind="ExternalInput").ap()
    convbx = nc.dram_tensor("convbx", [128, HH], F32, kind="ExternalInput").ap()
    convbb = nc.dram_tensor("convbb", [64, 2], F32, kind="ExternalInput").ap()
    dtbias = nc.dram_tensor("dtbias", [HH, 1], F32, kind="ExternalInput").ap()
    alog = nc.dram_tensor("alog", [HH, 1], F32, kind="ExternalInput").ap()
    dvecr = nc.dram_tensor("dvecr", [1, HH], F32, kind="ExternalInput").ap()
    ehot = nc.dram_tensor("ehot", [HH, TT], F32, kind="ExternalInput").ap()
    l0mask = nc.dram_tensor("l0mask", [128, 512], F32, kind="ExternalInput").ap()
    identb = nc.dram_tensor("identb", [128, 128], BF16, kind="ExternalInput").ap()
    identf = nc.dram_tensor("identf", [128, 128], F32, kind="ExternalInput").ap()
    yss = nc.dram_tensor("yss", [HH * 128, TT], BF16, kind="ExternalOutput").ap()

    p = Prog(nc)
    sE = p.sem("sE")
    sV = p.sem("sV")
    sP = p.sem("sP")
    xl = Stream(p, "xl", 2)
    cl = Stream(p, "cl", 1)
    ot = Stream(p, "ot", 2)

    xraw = p.sb("xraw", [128, 2 * TT], BF16)
    xcv = p.sb("xcv", [128, 16 * TT], BF16)
    bcv = p.sb("bcv", [64, 2 * TT], BF16)
    cacc = p.sb("cacc", [128, 2 * TT], F32)
    wxc = p.sb("wxc", [128, HH * 4], F32)
    wbc = p.sb("wbc", [64, 8], F32)
    bxc = p.sb("bxc", [128, HH], F32)
    bbc = p.sb("bbc", [64, 2], F32)
    dtw = p.sb("dtw", [HH, 6 * TT], F32)
    dtb = p.sb("dtb", [HH, 1], F32)
    alo = p.sb("alo", [HH, 1], F32)
    expa = p.sb("expa", [HH, 2], F32)
    cdm = p.sb("cdm", [HH, 8], F32)
    onesf16 = p.sb("onesf16", [HH, 256], F32)
    onesrow = p.sb("onesrow", [1, 128], F32)
    dvr = p.sb("dvr", [1, HH], F32)
    c1 = p.sb("c1", [128, 1], F32)
    ehsb = p.sb("ehsb", [HH, TT], F32)
    msk = p.sb("msk", [128, 512], F32)
    idb = p.sb("idb", [128, 128], BF16)
    idf = p.sb("idf", [128, 128], F32)
    dtT = p.sb("dtT", [128, HH * HH], F32)
    ncT = p.sb("ncT", [128, HH * HH], F32)
    ecT = p.sb("ecT", [128, HH * HH], F32)
    dsT = p.sb("dsT", [128, HH * HH], F32)
    drep = p.sb("drep", [128, HH], F32)
    cdrep = p.sb("cdrep", [128, HH * 8], F32)
    scs = p.sb("scs", [128, 512], BF16)
    btb = p.sb("btb", [128, 2 * 64], BF16)
    xts = p.sb("xts", [128, 2 * 128], BF16)
    xdts = p.sb("xdts", [128, 2 * 128], BF16)
    bds = p.sb("bds", [128, 2 * 64], BF16)
    ltb = p.sb("ltb", [128, 2 * 256], F32)
    ltb2 = p.sb("ltb2", [128, 2 * 256], F32)
    ssb = p.sb("ssb", [128, 2 * 256], BF16)
    tmp1 = p.sb("tmp1", [128, 128], F32)
    tmp2 = p.sb("tmp2", [128, 128], F32)
    ybf = p.sb("ybf", [128, 128], BF16)
    Sst = p.sb("Sst", [128, HH * 2 * 128], F32)
    prevb = p.sb("prevb", [64, HH * 128], BF16)
    ystage = p.sb("ystage", [128, 2 * 256], BF16)

    scA = p.ps("scA", [128, 512], F32)
    T1 = p.ps("T1", [128, 256], F32)
    tpb = p.ps("tpb", [128, 256], BF16)
    ypd = [p.ps(f"ypd{i}", [128, 128], F32) for i in range(2)]
    yof = [p.ps(f"yof{i}", [128, 128], F32) for i in range(2)]
    stp = p.ps("stp", [128, 128], F32)

    xtok = [xl.tok() for _ in range(HH + 2)]
    NCL = 12
    for _ in range(NCL):
        cl.tok()
    cl_all = (cl.sems[0], cl.sems[0].n)
    ytok = [ot.tok() for _ in range(HH * NCH)]
    sil = [sE.inc() for _ in range(18)]
    e_exp = sE.inc()
    e_ln = sE.inc()
    e_expa = sE.inc()
    e_nega = sE.inc()
    e_ds = [sE.inc() for _ in range(NCH)]
    e_ecum = sE.inc()
    e_cd = sE.inc()
    e_cdrep = [sE.inc() for _ in range(HH)]
    e_lt = [[0] * HH for _ in range(NCH)]
    e_yo = [[[0] * 2 for _ in range(HH)] for _ in range(NCH)]
    for c in range(NCH):
        for h in range(HH):
            e_lt[c][h] = sE.inc(2)
            for sb_ in range(2):
                e_yo[c][h][sb_] = sE.inc()
    v_init = sV.inc()
    v_conv = [sV.inc() for _ in range(18)]
    v_z = sV.inc()
    v_mz = sV.inc()
    v_dA = sV.inc()
    v_cum = sV.inc()
    v_tev = [sV.inc() for _ in range(4 * HH)]
    v_drep = sV.inc()
    v_sc = [[0] * 2 for _ in range(NCH)]
    v_bt = [[0] * 2 for _ in range(NCH)]
    v_ss = [[[0] * 2 for _ in range(HH)] for _ in range(NCH)]
    v_xd = [[[0] * 2 for _ in range(HH)] for _ in range(NCH)]
    v_bd = [[[0] * 2 for _ in range(HH)] for _ in range(NCH)]
    v_ye = [[[0] * 2 for _ in range(HH)] for _ in range(NCH)]
    v_yt = [[[0] * 2 for _ in range(HH)] for _ in range(NCH)]
    v_pb = [[0] * HH for _ in range(NCH)]
    v_su = [[0] * HH for _ in range(NCH)]
    for c in range(NCH):
        for zi in range(2):
            v_sc[c][zi] = sV.inc()
            v_bt[c][zi] = sV.inc()
        for h in range(HH):
            v_pb[c][h] = sV.inc()
            for zi in range(2):
                v_ss[c][h][zi] = sV.inc()
            for zi in range(2):
                v_ss[c][h][zi] = (v_ss[c][h][zi], sV.inc())
            for zi in range(2):
                v_xd[c][h][zi] = sV.inc(2)
                v_bd[c][h][zi] = sV.inc()
            for sb_ in range(2):
                v_ye[c][h][sb_] = sV.inc(2)
                v_yt[c][h][sb_] = sV.inc()
            v_su[c][h] = sV.inc()
    p_tr = [sP.inc() for _ in range(4 * HH)]
    p_drep = sP.inc()
    p_cdrep = [sP.inc() for _ in range(HH)]
    p_sc = [[0] * 2 for _ in range(NCH)]
    p_bt = [[0] * 2 for _ in range(NCH)]
    p_t1 = [[0] * HH for _ in range(NCH)]
    p_xt = [[[0] * 2 for _ in range(HH)] for _ in range(NCH)]
    p_yd = [[[0] * 2 for _ in range(HH)] for _ in range(NCH)]
    p_yo = [[[0] * 2 for _ in range(HH)] for _ in range(NCH)]
    p_st = [[0] * HH for _ in range(NCH)]
    p_yt = [[[0] * 2 for _ in range(HH)] for _ in range(NCH)]
    for c in range(NCH):
        for zi in range(2):
            p_sc[c][zi] = sP.inc()
            p_bt[c][zi] = sP.inc()
        for h in range(HH):
            p_t1[c][h] = sP.inc()
            for zi in range(2):
                p_xt[c][h][zi] = sP.inc()
            for sb_ in range(2):
                p_yd[c][h][sb_] = sP.inc()
                p_yo[c][h][sb_] = sP.inc()
            p_st[c][h] = sP.inc()
            for sb_ in range(2):
                p_yt[c][h][sb_] = sP.inc()

    def g_all(g):
        for ap_, dst in [(convwx, wxc), (convwb, wbc), (convbx, bxc),
                         (convbb, bbc), (dtbias, dtb), (alog, alo),
                         (ehot, ehsb), (l0mask, msk), (identb, idb),
                         (identf, idf), (dtrr, dtw)]:
            g.dma_start(out=dst[:, 0:ap_.shape[1]] if dst is dtw else dst[:],
                        in_=ap_).then_inc(cl.sems[0].h, 16)
        g.dma_start(out=dvr[:], in_=dvecr).then_inc(cl.sems[0].h, 16)
        for ci in range(HH):
            if ci >= 2:
                g.wait_ge(sV.h, v_conv[ci - 2])
            g.dma_start(out=xraw[:, (ci % 2) * TT:(ci % 2 + 1) * TT],
                        in_=xpart[ci * 128:(ci + 1) * 128, :]).then_inc(xtok[ci][0].h, 16)
        g.wait_ge(sV.h, v_conv[14])
        g.dma_start(out=xraw[0:64, 0:TT], in_=bin_).then_inc(xtok[HH][0].h, 16)
        g.wait_ge(sV.h, v_conv[15])
        g.dma_start(out=xraw[0:64, TT:2 * TT], in_=cin_).then_inc(xtok[HH + 1][0].h, 16)
    p.add("gpsimd", g_all)

    def s_all(s):
        for c in range(NCH):
            for h in range(HH):
                i = c * HH + h
                s.wait_ge(sV.h, v_yt[c][h][1])
                s.dma_start(out=yss[h * 128:(h + 1) * 128, c * 256:(c + 1) * 256],
                            in_=ystage[:, (i % 2) * 256:(i % 2 + 1) * 256]
                            ).then_inc(ytok[i][0].h, 16)
        for stx in (xl, cl, ot):
            stx.barrier(s)
        s.wait_ge(sV.h, sV.n)
        s.wait_ge(sP.h, sP.n)
    p.add("sync", s_all)

    def e_all(sc):
        wtok(sc, cl_all)
        for ci in range(18):
            sc.wait_ge(sV.h, v_conv[ci])
            if ci < HH:
                sc.activation(out=xcv[:, ci * TT:(ci + 1) * TT],
                              in_=cacc[:, (ci % 2) * TT:(ci % 2 + 1) * TT],
                              func=AF.Silu).then_inc(sE.h, 1)
            else:
                j = ci - HH
                sc.activation(out=bcv[0:64, j * TT:(j + 1) * TT],
                              in_=cacc[0:64, (ci % 2) * TT:(ci % 2 + 1) * TT],
                              func=AF.Silu).then_inc(sE.h, 1)
        sc.wait_ge(sV.h, v_z)
        sc.activation(out=dtw[:, 4 * TT:5 * TT], in_=dtw[:, 3 * TT:4 * TT],
                      func=AF.Exp, scale=-1.0).then_inc(sE.h, 1)
        sc.activation(out=dtw[:, 2 * TT:3 * TT], in_=dtw[:, 4 * TT:5 * TT],
                      func=AF.Ln, bias=c1[0:HH, 0:1]).then_inc(sE.h, 1)
        sc.activation(out=expa[:, 0:1], in_=alo[:], func=AF.Exp).then_inc(sE.h, 1)
        sc.activation(out=expa[:, 1:2], in_=expa[:, 0:1], func=AF.Copy,
                      scale=-1.0).then_inc(sE.h, 1)
        sc.wait_ge(sV.h, v_cum)
        for c in range(NCH):
            sl = slice(TT + c * 256, TT + (c + 1) * 256)
            sc.activation(out=dtw[:, 2 * TT + c * 256:2 * TT + (c + 1) * 256],
                          in_=dtw[:, sl], func=AF.Exp, scale=-1.0,
                          bias=dtw[:, TT + c * 256 + 255:TT + c * 256 + 256]
                          ).then_inc(sE.h, 1)
        sc.activation(out=dtw[:, 3 * TT:4 * TT], in_=dtw[:, TT:2 * TT],
                      func=AF.Exp).then_inc(sE.h, 1)
        sc.activation(out=cdm[:, 0:8],
                      in_=dtw[:, TT + 255:2 * TT:256],
                      func=AF.Exp).then_inc(sE.h, 1)
        for h in range(HH):
            sc.wait_ge(sP.h, p_cdrep[h])
            sc.activation(out=cdrep[:, h * 8:(h + 1) * 8],
                          in_=stp[0:128, 0:8], func=AF.Copy).then_inc(sE.h, 1)
        for c in range(NCH):
            for h in range(HH):
                if c > 0 or h > 0:
                    ph, pc = (h - 1, c) if h > 0 else (HH - 1, c - 1)
                    sc.wait_ge(sV.h, v_ss[pc][ph][1][1])
                for zi in range(2):
                    sc.wait_ge(sV.h, v_ss[c][h][zi][0])
                    sc.activation(out=ltb2[:, zi * 256:(zi + 1) * 256],
                                  in_=ltb[:, zi * 256:(zi + 1) * 256],
                                  func=AF.Exp).then_inc(sE.h, 1)
                for sb_ in range(2):
                    sc.wait_ge(sP.h, p_yo[c][h][sb_])
                    if c > 0 or h > 0 or sb_ > 0:
                        if sb_ > 0:
                            sc.wait_ge(sV.h, v_ye[c][h][0])
                        else:
                            ph, pc = (h - 1, c) if h > 0 else (HH - 1, c - 1)
                            sc.wait_ge(sV.h, v_ye[pc][ph][1])
                    sc.activation(out=tmp1[:], in_=yof[sb_][:], func=AF.Copy,
                                  scale=ecT[:, (2 * c + sb_) * HH + h:(2 * c + sb_) * HH + h + 1]
                                  ).then_inc(sE.h, 1)
    p.add("scalar", e_all)

    def v_all(v):
        v.memset(c1[:], 1.0)
        v.memset(onesrow[:], 1.0)
        v.memset(onesf16[:], 1.0)
        v.memset(Sst[:], 0.0)
        v.memset(prevb[:], 0.0).then_inc(sV.h, 1)
        wtok(v, cl_all)
        for ci in range(18):
            wtok(v, xtok[ci])
            if ci >= 2:
                v.wait_ge(sE.h, sil[ci - 2])
            a = ci % 2
            if ci < HH:
                raw = xraw[:, a * TT:(a + 1) * TT]
                ca = cacc[:, a * TT:(a + 1) * TT]
                w = wxc[:, ci * 4:(ci + 1) * 4]
                b = bxc[:, ci:ci + 1]
            else:
                j0 = ci - HH
                raw = xraw[0:64, j0 * TT:(j0 + 1) * TT]
                ca = cacc[0:64, (ci % 2) * TT:(ci % 2 + 1) * TT]
                w = wbc[0:64, j0 * 4:(j0 + 1) * 4]
                b = bbc[0:64, j0:j0 + 1]
            v.tensor_scalar(out=ca, in0=raw, scalar1=w[:, 3:4], scalar2=b,
                            op0=ALU.mult, op1=ALU.add)
            for j in range(3):
                k0 = 3 - j
                v.scalar_tensor_tensor(out=ca[:, k0:TT], in0=raw[:, 0:TT - k0],
                                       scalar=w[:, j:j + 1], in1=ca[:, k0:TT],
                                       op0=ALU.mult, op1=ALU.add)
            v.engine_nop().then_inc(sV.h, 1)
        v.tensor_scalar(out=dtw[:, TT:2 * TT], in0=dtw[:, 0:TT],
                        scalar1=dtb[:, 0:1], scalar2=None, op0=ALU.add)
        v.tensor_scalar(out=dtw[:, 2 * TT:3 * TT], in0=dtw[:, TT:2 * TT],
                        scalar1=-1.0, scalar2=None, op0=ALU.mult)
        v.tensor_max(out=dtw[:, 3 * TT:4 * TT], in0=dtw[:, 2 * TT:3 * TT],
                     in1=dtw[:, TT:2 * TT]).then_inc(sV.h, 1)
        v.wait_ge(sE.h, e_ln)
        v.tensor_scalar(out=dtw[:, 4 * TT:5 * TT], in0=dtw[:, TT:2 * TT],
                        scalar1=0.0, scalar2=None, op0=ALU.max)
        v.tensor_add(out=dtw[:, 5 * TT:6 * TT], in0=dtw[:, 4 * TT:5 * TT],
                     in1=dtw[:, 2 * TT:3 * TT]).then_inc(sV.h, 1)
        v.wait_ge(sE.h, e_nega)
        v.tensor_scalar(out=dtw[:, 0:TT], in0=dtw[:, 5 * TT:6 * TT],
                        scalar1=expa[:, 1:2], scalar2=None,
                        op0=ALU.mult).then_inc(sV.h, 1)
        last = None
        for c in range(NCH):
            last = v.tensor_tensor_scan(
                out=dtw[:, TT + c * 256:TT + (c + 1) * 256],
                data0=onesf16[:],
                data1=dtw[:, c * 256:(c + 1) * 256],
                initial=0.0, op0=ALU.mult, op1=ALU.add)
        last.then_inc(sV.h, 1)
        for i in range(4 * HH):
            v.wait_ge(sP.h, p_tr[i])
            st_, ts_ = i // HH, i % HH
            dst = (dtT, ncT, ecT, dsT)[st_]
            if st_ == 1:
                v.tensor_scalar(out=dst[:, ts_ * HH:(ts_ + 1) * HH], in0=T1[0:128, 0:HH],
                                scalar1=-1.0, scalar2=None,
                                op0=ALU.mult).then_inc(sV.h, 1)
            else:
                v.tensor_copy(out=dst[:, ts_ * HH:(ts_ + 1) * HH],
                              in_=T1[0:128, 0:HH]).then_inc(sV.h, 1)
        v.wait_ge(sP.h, p_drep)
        v.tensor_copy(out=drep[:], in_=T1[0:128, 0:HH]).then_inc(sV.h, 1)
        for c in range(NCH):
            for zi in range(2):
                v.wait_ge(sP.h, p_sc[c][zi])
                v.tensor_copy(out=scs[:, zi * 256:(zi + 1) * 256],
                              in_=scA[:, zi * 256:(zi + 1) * 256]).then_inc(sV.h, 1)
                v.wait_ge(sP.h, p_bt[c][zi])
                v.tensor_copy(out=btb[:, zi * 64:(zi + 1) * 64],
                              in_=tpb[0:128, 0:64]).then_inc(sV.h, 1)
            for h in range(HH):
                v.tensor_copy(out=prevb[:, h * 128:(h + 1) * 128],
                              in_=Sst[0:64, (h * 2 + (c % 2)) * 128:
                                      (h * 2 + (c % 2) + 1) * 128]).then_inc(sV.h, 1)
                v.wait_ge(sP.h, p_t1[c][h])
                for zi in range(2):
                    v.scalar_tensor_tensor(
                        out=ltb2[:, zi * 256:(zi + 1) * 256], in0=T1[:],
                        scalar=ncT[:, (2 * c + zi) * HH + h:(2 * c + zi) * HH + h + 1],
                        in1=msk[:, zi * 256:(zi + 1) * 256],
                        op0=ALU.add, op1=ALU.add)
                    v.tensor_scalar(out=ltb[:, zi * 256:(zi + 1) * 256],
                                    in0=ltb2[:, zi * 256:(zi + 1) * 256],
                                    scalar1=-80.0, scalar2=0.0,
                                    op0=ALU.max, op1=ALU.min).then_inc(sV.h, 1)
                for zi in range(2):
                    v.wait_ge(sE.h, e_lt[c][h] - (1 - zi))
                    v.tensor_mul(out=ssb[:, zi * 256:(zi + 1) * 256],
                                 in0=ltb2[:, zi * 256:(zi + 1) * 256],
                                 in1=scs[:, zi * 256:(zi + 1) * 256]).then_inc(sV.h, 1)
                for zi in range(2):
                    v.wait_ge(sP.h, p_xt[c][h][zi])
                    v.tensor_copy(out=xts[:, zi * 128:(zi + 1) * 128],
                                  in_=tpb[0:128, 0:128])
                    v.tensor_scalar(out=xdts[:, zi * 128:(zi + 1) * 128],
                                    in0=xts[:, zi * 128:(zi + 1) * 128],
                                    scalar1=dtT[:, (2 * c + zi) * HH + h:
                                                (2 * c + zi) * HH + h + 1],
                                    scalar2=None, op0=ALU.mult).then_inc(sV.h, 2)
                for zi in range(2):
                    v.tensor_scalar(out=bds[:, zi * 64:(zi + 1) * 64],
                                    in0=btb[:, zi * 64:(zi + 1) * 64],
                                    scalar1=dsT[:, (2 * c + zi) * HH + h:
                                                (2 * c + zi) * HH + h + 1],
                                    scalar2=None, op0=ALU.mult).then_inc(sV.h, 1)
                for sb_ in range(2):
                    v.wait_ge(sE.h, e_yo[c][h][sb_])
                    v.wait_ge(sP.h, p_yd[c][h][sb_])
                    v.tensor_add(out=tmp2[:], in0=tmp1[:], in1=ypd[sb_][:])
                    v.scalar_tensor_tensor(out=ybf[:],
                                           in0=xts[:, sb_ * 128:(sb_ + 1) * 128],
                                           scalar=drep[:, h:h + 1], in1=tmp2[:],
                                           op0=ALU.mult, op1=ALU.add).then_inc(sV.h, 2)
                    v.wait_ge(sP.h, p_yt[c][h][sb_])
                    i = c * HH + h
                    if i >= 2:
                        wtok(v, ytok[i - 2])
                    v.tensor_copy(out=ystage[:, (i % 2) * 256 + sb_ * 128:
                                             (i % 2) * 256 + (sb_ + 1) * 128],
                                  in_=tpb[0:128, 0:128]).then_inc(sV.h, 1)
                v.wait_ge(sP.h, p_st[c][h])
                v.wait_ge(sE.h, e_cdrep[h])
                v.scalar_tensor_tensor(
                    out=Sst[0:64, (h * 2 + ((c + 1) % 2)) * 128:
                            (h * 2 + ((c + 1) % 2) + 1) * 128],
                    in0=Sst[0:64, (h * 2 + (c % 2)) * 128:(h * 2 + (c % 2) + 1) * 128],
                    scalar=cdrep[0:64, h * 8 + c:h * 8 + c + 1], in1=stp[0:64, :],
                    op0=ALU.mult, op1=ALU.add).then_inc(sV.h, 1)
    p.add("vector", v_all)

    def t_all(t):
        wtok(t, cl_all)
        t.wait_ge(sV.h, v_cum)
        t.wait_ge(sE.h, e_cd)
        srcs = [dtw[:, 5 * TT:6 * TT], dtw[:, TT:2 * TT],
                dtw[:, 3 * TT:4 * TT], dtw[:, 2 * TT:3 * TT]]
        for i in range(4 * HH):
            st_, ts_ = i // HH, i % HH
            if i >= 1:
                t.wait_ge(sV.h, v_tev[i - 1])
            t.transpose(out=T1[0:128, 0:HH],
                        in_=srcs[st_][0:HH, ts_ * 128:(ts_ + 1) * 128],
                        identity=idf[0:HH, 0:HH]).then_inc(sP.h, 1)
        t.wait_ge(sV.h, v_tev[4 * HH - 1])
        t.matmul(T1[0:128, 0:HH], lhsT=onesrow[:], rhs=dvr[:],
                 start=True, stop=True).then_inc(sP.h, 1)
        for h in range(HH):
            if h == 0:
                t.wait_ge(sV.h, v_drep)
            else:
                t.wait_ge(sE.h, e_cdrep[h - 1])
            t.matmul(stp[:, 0:8], lhsT=ehsb[0:HH, h * 128:(h + 1) * 128],
                     rhs=cdm[:, 0:8], start=True, stop=True).then_inc(sP.h, 1)
        for c in range(NCH):
            for zi in range(2):
                if c > 0 or zi > 0:
                    t.wait_ge(sV.h, v_sc[c][zi - 1] if zi > 0 else v_sc[c - 1][1])
                t.matmul(scA[:, zi * 256:(zi + 1) * 256],
                         lhsT=bcv[0:64, c * 256 + zi * 128:c * 256 + (zi + 1) * 128],
                         rhs=bcv[0:64, TT + c * 256:TT + (c + 1) * 256],
                         start=True, stop=True).then_inc(sP.h, 1)
                t.wait_ge(sV.h, v_bt[c - 1][zi] if c > 0 else 0)
                t.wait_ge(sE.h, sil[17])
                t.transpose(out=tpb[0:128, 0:64],
                            in_=bcv[0:64, c * 256 + zi * 128:c * 256 + (zi + 1) * 128],
                            identity=idb[0:64, 0:64]).then_inc(sP.h, 1)
            for h in range(HH):
                if c > 0 or h > 0:
                    ph, pc = (h - 1, c) if h > 0 else (HH - 1, c - 1)
                    t.wait_ge(sV.h, v_ss[pc][ph][1][0])
                t.matmul(T1[:, 0:256], lhsT=ehsb[0:HH, h * 128:(h + 1) * 128],
                         rhs=dtw[:, TT + c * 256:TT + (c + 1) * 256],
                         start=True, stop=True).then_inc(sP.h, 1)
                for zi in range(2):
                    if h > 0:
                        t.wait_ge(sV.h, v_yt[c][h - 1][1])
                    else:
                        t.wait_ge(sV.h, v_bt[c][1])
                    t.wait_ge(sE.h, sil[h])
                    t.transpose(out=tpb[0:128, 0:128],
                                in_=xcv[:, h * TT + c * 256 + zi * 128:
                                        h * TT + c * 256 + (zi + 1) * 128],
                                identity=idb[:]).then_inc(sP.h, 1)
                    t.wait_ge(sV.h, v_xd[c][h][zi] - 1)
                for sb_ in range(2):
                    t.wait_ge(sV.h, v_ss[c][h][1][1])
                    t.wait_ge(sV.h, v_xd[c][h][1])
                    if c * HH + h >= 1:
                        ph, pc = (h - 1, c) if h > 0 else (HH - 1, c - 1)
                        t.wait_ge(sV.h, v_ye[pc][ph][sb_])
                    for zi in range(2):
                        t.matmul(ypd[sb_][:],
                                 lhsT=ssb[:, zi * 256 + sb_ * 128:
                                          zi * 256 + (sb_ + 1) * 128],
                                 rhs=xdts[:, zi * 128:(zi + 1) * 128],
                                 start=(zi == 0), stop=(zi == 1)).then_inc(sP.h, 0 if zi == 0 else 1)
                    t.wait_ge(sV.h, v_pb[c][h])
                    if c > 0 or h > 0:
                        ph, pc = (h - 1, c) if h > 0 else (HH - 1, c - 1)
                        t.wait_ge(sE.h, e_yo[pc][ph][sb_])
                    t.matmul(yof[sb_][:],
                             lhsT=bcv[0:64, TT + c * 256 + sb_ * 128:
                                      TT + c * 256 + (sb_ + 1) * 128],
                             rhs=prevb[:, h * 128:(h + 1) * 128],
                             start=True, stop=True).then_inc(sP.h, 1)
                t.wait_ge(sV.h, v_bd[c][h][1])
                if c * HH + h >= 1:
                    ph, pc = (h - 1, c) if h > 0 else (HH - 1, c - 1)
                    t.wait_ge(sV.h, v_su[pc][ph])
                for zi in range(2):
                    t.matmul(stp[0:64, :], lhsT=bds[:, zi * 64:(zi + 1) * 64],
                             rhs=xdts[:, zi * 128:(zi + 1) * 128],
                             start=(zi == 0), stop=(zi == 1)).then_inc(sP.h, 0 if zi == 0 else 1)
                for sb_ in range(2):
                    t.wait_ge(sV.h, v_ye[c][h][sb_])
                    t.transpose(out=tpb[0:128, 0:128], in_=ybf[:],
                                identity=idb[:]).then_inc(sP.h, 1)
                    t.wait_ge(sV.h, v_yt[c][h][sb_])
    p.add("tensor", t_all)

    p.run()
    return nc



def build_mla2():
    nc = _bass()
    TT = 2048
    NH = 8
    NPAN = 4
    NZ = 16
    NQi = nc.dram_tensor("NQ", [768, TT], BF16, kind="ExternalInput").ap()
    NKVi = nc.dram_tensor("NKV", [256, TT], BF16, kind="ExternalInput").ap()
    KRRi = nc.dram_tensor("KRR", [32, TT], BF16, kind="ExternalInput").ap()
    qnwi = nc.dram_tensor("qnw", [768, NH * 64], BF16, kind="ExternalInput").ap()
    qrwi = nc.dram_tensor("qrw", [768, NH * 32], BF16, kind="ExternalInput").ap()
    knwi = nc.dram_tensor("knw", [256, NH * 64], BF16, kind="ExternalInput").ap()
    vvwi = nc.dram_tensor("vvw", [256, NH * 64], BF16, kind="ExternalInput").ap()
    costq = nc.dram_tensor("costq", [128, 16 * 16], F32, kind="ExternalInput").ap()
    dmask = nc.dram_tensor("dmask", [128, 4 * 512], BF16, kind="ExternalInput").ap()
    identb = nc.dram_tensor("identb", [128, 128], BF16, kind="ExternalInput").ap()
    OUT = nc.dram_tensor("OUT", [NH * TT, 64], BF16, kind="ExternalOutput").ap()

    p = Prog(nc)
    sE = p.sem("sE")
    sV = p.sem("sV")
    sP = p.sem("sP")
    cl = Stream(p, "cl", 1)
    ot = Stream(p, "ot", 2)

    nqb = p.sb("nqb", [128, 6 * TT], BF16)
    nkb = p.sb("nkb", [128, 2 * TT], BF16)
    krb = p.sb("krb", [32, TT], BF16)
    krs = p.sb("krs", [32, TT], BF16)
    wqn = p.sb("wqn", [128, 6 * NH * 64], BF16)
    wqr = p.sb("wqr", [128, 6 * NH * 32], BF16)
    wkn = p.sb("wkn", [128, 2 * NH * 64], BF16)
    wvv = p.sb("wvv", [128, 2 * NH * 64], BF16)
    ctq = p.sb("ctq", [128, 256], F32)
    dmk = p.sb("dmk", [128, 4 * 512], BF16)
    idb = p.sb("idb", [128, 128], BF16)
    onesb = p.sb("onesb", [128, 1], BF16)
    onesrf = p.sb("onesrf", [1, 128], F32)
    qnf = p.sb("qnf", [64, TT], BF16)
    qns = p.sb("qns", [64, TT], BF16)
    qrf = p.sb("qrf", [32, TT], BF16)
    qrq = p.sb("qrq", [32, TT], BF16)
    knf = p.sb("knf", [64, TT], BF16)
    kns = p.sb("kns", [64, TT], BF16)
    vtt = p.sb("vtt", [128, NZ * 64], BF16)
    qrt = p.sb("qrt", [128, 2 * 32], F32)
    qro = p.sb("qro", [128, 2 * 32], BF16)
    r2row = p.sb("r2row", [1, 2 * TT], F32)
    brow = p.sb("brow", [1, TT], F32)
    rkm = p.sb("rkm", [1, 2], F32)
    bbc = p.sb("bbc", [128, 512], F32)
    ebuf = p.sb("ebuf", [128, NZ * 512], BF16)
    esc = p.sb("esc", [128, 2 * 512], F32)
    smrow = p.sb("smrow", [1, 512], F32)
    rcol = p.sb("rcol", [128, 8], F32)
    ostg = p.sb("ostg", [128, 2 * 64], BF16)

    pA = p.ps("pA", [64, 512], F32)
    pT = p.ps("pT", [32, 128], BF16)
    pms = p.ps("pms", [128, 512], F32)
    sc2 = [p.ps(f"sc2{a}", [128, 512], F32) for a in range(2)]
    smr = p.ps("smr", [1, 512], F32)
    pmisc = p.ps("pmisc", [1, 512], F32)
    pav = p.ps("pav", [128, 64], F32)

    def nzp(pan):
        return 4 * pan + 4

    NCL = 28
    for _ in range(NCL):
        cl.tok()
    cl_all = (cl.sems[0], cl.sems[0].n)
    otok = [[[None] * 4 for _ in range(NPAN)] for _ in range(NH)]
    for h in range(NH):
        for pan in range(NPAN):
            for ssub in range(4):
                otok[h][pan][ssub] = ot.tok()
    e_krs = sE.inc()
    e_qn = [[0] * 4 for _ in range(NH)]
    e_qsq = [[0] * 4 for _ in range(NH)]
    e_kn = [[0] * 4 for _ in range(NH)]
    e_ksq = [[0] * 4 for _ in range(NH)]
    e_rk = [0] * NH
    e_qrs = [0] * NH
    e_rq = [0] * NH
    e_exp = [[[0] * 16 for _ in range(NPAN)] for _ in range(NH)]
    for h in range(NH):
        for st in range(4):
            e_qn[h][st] = sE.inc()
        for st in range(4):
            e_qsq[h][st] = sE.inc()
        for st in range(4):
            e_kn[h][st] = sE.inc(2)
            e_ksq[h][st] = e_kn[h][st]
        e_rk[h] = sE.inc()
        e_qrs[h] = sE.inc()
        e_rq[h] = sE.inc()
        for pan in range(NPAN):
            for zi in range(nzp(pan)):
                e_exp[h][pan][zi] = sE.inc()
    v_init = sV.inc()
    v_qr = [[0] * 16 for _ in range(NH)]
    v_qrt = [[0] * 16 for _ in range(NH)]
    v_vt = [[0] * 16 for _ in range(NH)]
    v_rk = [[0] * 4 for _ in range(NH)]
    v_rkm = [0] * NH
    v_rq2 = [[0] * 4 for _ in range(NH)]
    v_rq = [0] * NH
    v_bb = [[0] * NPAN for _ in range(NH)]
    v_eb = [[[0] * 16 for _ in range(NPAN)] for _ in range(NH)]
    v_dm = [[[0] * 16 for _ in range(NPAN)] for _ in range(NH)]
    v_smr = [[0] * NPAN for _ in range(NH)]
    v_rc = [[[0] * 4 for _ in range(NPAN)] for _ in range(NH)]
    v_rcp = [[0] * NPAN for _ in range(NH)]
    v_av = [[[0] * 4 for _ in range(NPAN)] for _ in range(NH)]
    for h in range(NH):
        for i in range(16):
            v_qr[h][i] = sV.inc(5)
            v_qrt[h][i] = sV.inc()
        for i in range(16):
            v_vt[h][i] = sV.inc()
        for st in range(4):
            v_rk[h][st] = sV.inc()
        v_rkm[h] = sV.inc()
        for st in range(4):
            v_rq2[h][st] = sV.inc()
        v_rq[h] = sV.inc()
        for pan in range(NPAN):
            v_bb[h][pan] = sV.inc()
            for zi in range(nzp(pan)):
                v_eb[h][pan][zi] = sV.inc()
                if zi >= 4 * pan:
                    v_dm[h][pan][zi] = sV.inc()
            v_smr[h][pan] = sV.inc()
            for ssub in range(4):
                v_rc[h][pan][ssub] = sV.inc()
            v_rcp[h][pan] = sV.inc()
            for ssub in range(4):
                v_av[h][pan][ssub] = sV.inc()
    p_qn = [[0] * 4 for _ in range(NH)]
    p_qrp = [[0] * 16 for _ in range(NH)]
    p_qrt = [[0] * 16 for _ in range(NH)]
    p_kn = [[0] * 4 for _ in range(NH)]
    p_vt = [[0] * 16 for _ in range(NH)]
    p_rk = [[0] * 4 for _ in range(NH)]
    p_rq = [[0] * 4 for _ in range(NH)]
    p_bb = [[0] * NPAN for _ in range(NH)]
    p_sc = [[[0] * 16 for _ in range(NPAN)] for _ in range(NH)]
    p_sm = [[0] * NPAN for _ in range(NH)]
    p_tp = [[[0] * 4 for _ in range(NPAN)] for _ in range(NH)]
    p_av = [[[0] * 4 for _ in range(NPAN)] for _ in range(NH)]
    for h in range(NH):
        for st in range(4):
            p_qn[h][st] = sP.inc()
        for i in range(16):
            p_qrp[h][i] = sP.inc()
            p_qrt[h][i] = sP.inc()
        for st in range(4):
            p_kn[h][st] = sP.inc()
        for i in range(16):
            p_vt[h][i] = sP.inc()
        for st in range(4):
            p_rk[h][st] = sP.inc()
        for st in range(4):
            p_rq[h][st] = sP.inc()
        for pan in range(NPAN):
            p_bb[h][pan] = sP.inc()
            for zi in range(nzp(pan)):
                p_sc[h][pan][zi] = sP.inc()
            p_sm[h][pan] = sP.inc()
            for ssub in range(4):
                p_tp[h][pan][ssub] = sP.inc()
            for ssub in range(4):
                p_av[h][pan][ssub] = sP.inc()

    def g_all(g):
        for i in range(6):
            g.dma_start(out=nqb[:, i * TT:(i + 1) * TT],
                        in_=NQi[i * 128:(i + 1) * 128, :]).then_inc(cl.sems[0].h, 16)
        for i in range(2):
            g.dma_start(out=nkb[:, i * TT:(i + 1) * TT],
                        in_=NKVi[i * 128:(i + 1) * 128, :]).then_inc(cl.sems[0].h, 16)
        g.dma_start(out=krb[:], in_=KRRi).then_inc(cl.sems[0].h, 16)
        for w_in, w_sb, nch, wid in [(qnwi, wqn, 6, NH * 64), (qrwi, wqr, 6, NH * 32),
                                     (knwi, wkn, 2, NH * 64), (vvwi, wvv, 2, NH * 64)]:
            for i in range(nch):
                g.dma_start(out=w_sb[:, i * wid:(i + 1) * wid],
                            in_=w_in[i * 128:(i + 1) * 128, :])
            g.engine_nop().then_inc(cl.sems[0].h, 16) if False else None
            g.dma_start(out=ctq[:, 0:1], in_=costq[:, 0:1]).then_inc(cl.sems[0].h, 16) if False else None
    p.add("gpsimd", lambda g: None)
    p.q["gpsimd"].pop()

    def g_all2(g):
        for i in range(6):
            g.dma_start(out=nqb[:, i * TT:(i + 1) * TT],
                        in_=NQi[i * 128:(i + 1) * 128, :]).then_inc(cl.sems[0].h, 16)
        g.dma_start(out=nkb[:, 0:TT], in_=NKVi[0:128, :]).then_inc(cl.sems[0].h, 16)
        g.dma_start(out=nkb[:, TT:2 * TT], in_=NKVi[128:256, :]).then_inc(cl.sems[0].h, 16)
        g.dma_start(out=krb[:], in_=KRRi).then_inc(cl.sems[0].h, 16)
        for i in range(6):
            g.dma_start(out=wqn[:, i * NH * 64:(i + 1) * NH * 64],
                        in_=qnwi[i * 128:(i + 1) * 128, :]).then_inc(cl.sems[0].h, 16)
            g.dma_start(out=wqr[:, i * NH * 32:(i + 1) * NH * 32],
                        in_=qrwi[i * 128:(i + 1) * 128, :]).then_inc(cl.sems[0].h, 16)
        for i in range(2):
            g.dma_start(out=wkn[:, i * NH * 64:(i + 1) * NH * 64],
                        in_=knwi[i * 128:(i + 1) * 128, :]).then_inc(cl.sems[0].h, 16)
            g.dma_start(out=wvv[:, i * NH * 64:(i + 1) * NH * 64],
                        in_=vvwi[i * 128:(i + 1) * 128, :]).then_inc(cl.sems[0].h, 16)
        g.dma_start(out=ctq[:], in_=costq).then_inc(cl.sems[0].h, 16)
        g.dma_start(out=dmk[:], in_=dmask).then_inc(cl.sems[0].h, 16)
        g.dma_start(out=idb[:], in_=identb).then_inc(cl.sems[0].h, 16)

    p.add("gpsimd", g_all2)

    def s_all(s):
        for h in range(NH):
            for pan in range(NPAN):
                for ssub in range(4):
                    s.wait_ge(sV.h, v_av[h][pan][ssub])
                    i = (h * NPAN + pan) * 4 + ssub
                    s.dma_start(out=OUT[h * TT + pan * 512 + ssub * 128:
                                        h * TT + pan * 512 + (ssub + 1) * 128, :],
                                in_=ostg[:, (i % 2) * 64:(i % 2 + 1) * 64]
                                ).then_inc(otok[h][pan][ssub][0].h, 16)
        for stx in (cl, ot):
            stx.barrier(s)
        s.wait_ge(sV.h, sV.n)
        s.wait_ge(sP.h, sP.n)
    p.add("sync", s_all)

    def e_all(sc):
        wtok(sc, cl_all)
        sc.activation(out=krs[:], in_=krb[:], func=AF.Square).then_inc(sE.h, 1)
        for h in range(NH):
            for st in range(4):
                sc.wait_ge(sP.h, p_qn[h][st])
                sc.activation(out=qnf[:, st * 512:(st + 1) * 512], in_=pA[:],
                              func=AF.Copy, scale=0.10206207261596575
                              ).then_inc(sE.h, 1)
            for st in range(4):
                sc.activation(out=qns[:, st * 512:(st + 1) * 512],
                              in_=qnf[:, st * 512:(st + 1) * 512],
                              func=AF.Square).then_inc(sE.h, 1)
            for st in range(4):
                sc.wait_ge(sP.h, p_kn[h][st])
                sc.activation(out=knf[:, st * 512:(st + 1) * 512], in_=pA[:],
                              func=AF.Copy).then_inc(sE.h, 1)
                sc.activation(out=kns[:, st * 512:(st + 1) * 512],
                              in_=knf[:, st * 512:(st + 1) * 512],
                              func=AF.Square).then_inc(sE.h, 1)
            sc.wait_ge(sV.h, v_rk[h][3])
            sc.activation(out=r2row[0:1, 0:TT], in_=r2row[0:1, TT:2 * TT],
                          func=AF.Sqrt).then_inc(sE.h, 1)
            sc.wait_ge(sV.h, v_qrt[h][15])
            sc.activation(out=qrq[:], in_=qrf[:], func=AF.Square).then_inc(sE.h, 1)
            sc.wait_ge(sV.h, v_rq2[h][3])
            sc.wait_ge(sV.h, v_rkm[h])
            sc.activation(out=r2row[0:1, 0:TT], in_=r2row[0:1, TT:2 * TT],
                          func=AF.Sqrt).then_inc(sE.h, 1)
            for pan in range(NPAN):
                for zi in range(nzp(pan)):
                    sc.wait_ge(sV.h, v_eb[h][pan][zi])
                    if zi == 0 and (pan > 0 or h > 0):
                        ph2, pp2 = (pan - 1, h) if pan > 0 else (NPAN - 1, h - 1)
                        sc.wait_ge(sP.h, p_av[pp2][ph2][3])
                    sc.activation(out=ebuf[:, zi * 512:(zi + 1) * 512],
                                  in_=esc[:, (zi % 2) * 512:(zi % 2 + 1) * 512],
                                  func=AF.Exp).then_inc(sE.h, 1)
    p.add("scalar", e_all)

    def v_all(v):
        v.memset(onesb[:], 1.0)
        v.memset(onesrf[:], 1.0).then_inc(sV.h, 1)
        for h in range(NH):
            for i in range(16):
                v.wait_ge(sP.h, p_qrp[h][i])
                a = i % 2
                v.tensor_copy(out=qrt[:, a * 32:(a + 1) * 32],
                              in_=pms[0:128, 0:32]).then_inc(sV.h, 1)
                v.wait_ge(sV.h, v_qr[h][i] - 4)
                e = qrt[:, a * 32:(a + 1) * 32:2]
                o = qrt[:, a * 32 + 1:(a + 1) * 32:2]
                c_ = ctq[:, i * 16:(i + 1) * 16]
                t1 = esc[:, 0:16]
                t2 = esc[:, 16:32]
                v.tensor_sub(out=t1, in0=e, in1=o)
                v.tensor_add(out=t2, in0=e, in1=o).then_inc(sV.h, 1)
                v.wait_ge(sV.h, v_qr[h][i] - 3)
                v.tensor_mul(out=qro[:, a * 32:a * 32 + 16], in0=t1, in1=c_)
                v.tensor_mul(out=qro[:, a * 32 + 16:(a + 1) * 32], in0=t2, in1=c_
                             ).then_inc(sV.h, 3)
                v.wait_ge(sP.h, p_qrt[h][i])
                v.tensor_copy(out=qrf[:, i * 128:(i + 1) * 128],
                              in_=pT[:]).then_inc(sV.h, 1)
            for i in range(16):
                v.wait_ge(sP.h, p_vt[h][i])
                v.tensor_copy(out=vtt[:, i * 64:(i + 1) * 64],
                              in_=pms[0:128, 0:64]).then_inc(sV.h, 1)
            for st in range(4):
                v.wait_ge(sP.h, p_rk[h][st])
                v.tensor_copy(out=r2row[0:1, TT + st * 512:TT + (st + 1) * 512],
                              in_=pmisc[:]).then_inc(sV.h, 1)
            v.wait_ge(sE.h, e_rk[h])
            v.reduce_max(out=rkm[0:1, 0:1], in_=r2row[0:1, 0:TT],
                         axis=AX.X).then_inc(sV.h, 1)
            for st in range(4):
                v.wait_ge(sP.h, p_rq[h][st])
                v.tensor_copy(out=r2row[0:1, TT + st * 512:TT + (st + 1) * 512],
                              in_=pmisc[:]).then_inc(sV.h, 1)
            v.wait_ge(sE.h, e_rq[h])
            v.wait_ge(sV.h, v_rkm[h])
            v.tensor_scalar(out=brow[0:1, 0:TT], in0=r2row[0:1, 0:TT],
                            scalar1=rkm[0:1, 0:1], scalar2=None,
                            op0=ALU.mult).then_inc(sV.h, 1)
            for pan in range(NPAN):
                v.wait_ge(sP.h, p_bb[h][pan])
                v.tensor_copy(out=bbc[:], in_=pms[:]).then_inc(sV.h, 1)
                for zi in range(nzp(pan)):
                    v.wait_ge(sP.h, p_sc[h][pan][zi])
                    if zi >= 2:
                        v.wait_ge(sE.h, e_exp[h][pan][zi - 2])
                    elif pan > 0:
                        v.wait_ge(sE.h, e_exp[h][pan - 1][nzp(pan - 1) - 2 + zi])
                    elif h > 0:
                        v.wait_ge(sE.h, e_exp[h - 1][NPAN - 1][nzp(NPAN - 1) - 2 + zi])
                    v.scalar_tensor_tensor(out=esc[:, (zi % 2) * 512:(zi % 2 + 1) * 512],
                                           in0=sc2[zi % 2][:], scalar=1.0, in1=bbc[:],
                                           op0=ALU.mult, op1=ALU.subtract
                                           ).then_inc(sV.h, 1)
                    if zi >= 4 * pan:
                        v.wait_ge(sE.h, e_exp[h][pan][zi])
                        v.tensor_mul(out=ebuf[:, zi * 512:(zi + 1) * 512],
                                     in0=ebuf[:, zi * 512:(zi + 1) * 512],
                                     in1=dmk[:, (zi - 4 * pan) * 512:
                                             (zi - 4 * pan + 1) * 512]
                                     ).then_inc(sV.h, 1)
                v.wait_ge(sP.h, p_sm[h][pan])
                v.tensor_copy(out=smrow[:], in_=smr[:]).then_inc(sV.h, 1)
                for ssub in range(4):
                    v.wait_ge(sP.h, p_tp[h][pan][ssub])
                    v.tensor_copy(out=rcol[:, ssub:ssub + 1],
                                  in_=pav[0:128, 0:1]).then_inc(sV.h, 1)
                v.wait_ge(sV.h, v_rc[h][pan][3])
                v.reciprocal(out=rcol[:, 4:8], in_=rcol[:, 0:4]).then_inc(sV.h, 1)
                v.wait_ge(sV.h, v_rcp[h][pan])
                for ssub in range(4):
                    v.wait_ge(sP.h, p_av[h][pan][ssub])
                    i = (h * NPAN + pan) * 4 + ssub
                    if i >= 2:
                        pi = i - 2
                        hh2, rr = pi // 16, pi % 16
                        wtok(v, otok[hh2][rr // 4][rr % 4])
                    v.tensor_scalar(out=ostg[:, (i % 2) * 64:(i % 2 + 1) * 64],
                                    in0=pav[:], scalar1=rcol[:, 4 + ssub:5 + ssub],
                                    scalar2=None, op0=ALU.mult).then_inc(sV.h, 1)
    p.add("vector", v_all)

    def t_all(t):
        wtok(t, cl_all)
        t.wait_ge(sV.h, v_init)
        for h in range(NH):
            for st in range(4):
                if h > 0 and st == 0:
                    t.wait_ge(sE.h, e_qsq[h - 1][3])
                if st > 0:
                    t.wait_ge(sE.h, e_qn[h][st - 1])
                elif h > 0:
                    t.wait_ge(sE.h, e_qn[h - 1][3])
                last = None
                for k in range(6):
                    last = t.matmul(pA[:],
                                    lhsT=wqn[:, k * NH * 64 + h * 64:
                                             k * NH * 64 + (h + 1) * 64],
                                    rhs=nqb[:, k * TT + st * 512:k * TT + (st + 1) * 512],
                                    start=(k == 0), stop=(k == 5))
                last.then_inc(sP.h, 1)
            for i in range(16):
                if i >= 1:
                    t.wait_ge(sV.h, v_qr[h][i - 1] - 4)
                elif h > 0:
                    t.wait_ge(sV.h, v_qr[h - 1][15] - 4)
                last = None
                for k in range(6):
                    last = t.matmul(pms[0:128, 0:32],
                                    lhsT=nqb[:, k * TT + i * 128:k * TT + (i + 1) * 128],
                                    rhs=wqr[:, k * NH * 32 + h * 32:
                                            k * NH * 32 + (h + 1) * 32],
                                    start=(k == 0), stop=(k == 5))
                last.then_inc(sP.h, 1)
                t.wait_ge(sV.h, v_qr[h][i])
                if i >= 1:
                    t.wait_ge(sV.h, v_qrt[h][i - 1])
                elif h > 0:
                    t.wait_ge(sV.h, v_qrt[h - 1][15])
                t.transpose(out=pT[:], in_=qro[:, (i % 2) * 32:(i % 2 + 1) * 32],
                            identity=idb[:]).then_inc(sP.h, 1)
            for st in range(4):
                if st == 0:
                    t.wait_ge(sE.h, e_qn[h][3])
                if st > 0:
                    t.wait_ge(sE.h, e_kn[h][st - 1])
                last = None
                for k in range(2):
                    last = t.matmul(pA[:],
                                    lhsT=wkn[:, k * NH * 64 + h * 64:
                                             k * NH * 64 + (h + 1) * 64],
                                    rhs=nkb[:, k * TT + st * 512:k * TT + (st + 1) * 512],
                                    start=(k == 0), stop=(k == 1))
                last.then_inc(sP.h, 1)
            for i in range(16):
                if i >= 1:
                    t.wait_ge(sV.h, v_vt[h][i - 1])
                else:
                    t.wait_ge(sV.h, v_qr[h][15] - 4)
                    if h > 0:
                        t.wait_ge(sV.h, v_vt[h - 1][15])
                last = None
                for k in range(2):
                    last = t.matmul(pms[0:128, 0:64],
                                    lhsT=nkb[:, k * TT + i * 128:k * TT + (i + 1) * 128],
                                    rhs=wvv[:, k * NH * 64 + h * 64:
                                            k * NH * 64 + (h + 1) * 64],
                                    start=(k == 0), stop=(k == 1))
                last.then_inc(sP.h, 1)
            for st in range(4):
                t.wait_ge(sE.h, e_ksq[h][st])
                t.wait_ge(sE.h, e_krs)
                if st > 0:
                    t.wait_ge(sV.h, v_rk[h][st - 1])
                elif h > 0:
                    t.wait_ge(sV.h, v_rq2[h - 1][3])
                t.matmul(pmisc[:], lhsT=onesb[0:64, :],
                         rhs=kns[:, st * 512:(st + 1) * 512],
                         start=True, stop=False)
                t.matmul(pmisc[:], lhsT=onesb[0:32, :],
                         rhs=krs[:, st * 512:(st + 1) * 512],
                         start=False, stop=True).then_inc(sP.h, 1)
            for st in range(4):
                t.wait_ge(sE.h, e_qsq[h][st])
                t.wait_ge(sE.h, e_qrs[h])
                if st > 0:
                    t.wait_ge(sV.h, v_rq2[h][st - 1])
                else:
                    t.wait_ge(sV.h, v_rk[h][3])
                t.matmul(pmisc[:], lhsT=onesb[0:64, :],
                         rhs=qns[:, st * 512:(st + 1) * 512],
                         start=True, stop=False)
                t.matmul(pmisc[:], lhsT=onesb[0:32, :],
                         rhs=qrq[:, st * 512:(st + 1) * 512],
                         start=False, stop=True).then_inc(sP.h, 1)
            for pan in range(NPAN):
                t.wait_ge(sV.h, v_rq[h])
                if h > 0 or pan > 0:
                    ph2, pp2 = (pan - 1, h) if pan > 0 else (NPAN - 1, h - 1)
                    t.wait_ge(sV.h, v_bb[pp2][ph2])
                t.matmul(pms[:], lhsT=onesrf[:],
                         rhs=brow[0:1, pan * 512:(pan + 1) * 512],
                         start=True, stop=True).then_inc(sP.h, 1)
                for zi in range(nzp(pan)):
                    a = zi % 2
                    if zi >= 2:
                        t.wait_ge(sV.h, v_eb[h][pan][zi - 2])
                    elif pan > 0:
                        t.wait_ge(sV.h, v_eb[h][pan - 1][nzp(pan - 1) - 2 + zi])
                    elif h > 0:
                        t.wait_ge(sV.h, v_eb[h - 1][NPAN - 1][nzp(NPAN - 1) - 2 + zi])
                    t.matmul(sc2[a][:],
                             lhsT=knf[:, zi * 128:(zi + 1) * 128],
                             rhs=qnf[:, pan * 512:(pan + 1) * 512],
                             start=True, stop=False)
                    t.matmul(sc2[a][:],
                             lhsT=krb[:, zi * 128:(zi + 1) * 128],
                             rhs=qrf[:, pan * 512:(pan + 1) * 512],
                             start=False, stop=True).then_inc(sP.h, 1)
                last = None
                for zi in range(nzp(pan)):
                    if zi < 4 * pan:
                        t.wait_ge(sE.h, e_exp[h][pan][zi])
                    else:
                        t.wait_ge(sV.h, v_dm[h][pan][zi])
                    last = t.matmul(smr[:], lhsT=onesb[:],
                                    rhs=ebuf[:, zi * 512:(zi + 1) * 512],
                                    start=(zi == 0), stop=(zi == nzp(pan) - 1))
                last.then_inc(sP.h, 1)
                for ssub in range(4):
                    t.wait_ge(sV.h, v_smr[h][pan])
                    if True:
                        pass
                    t.transpose(out=pav[0:128, 0:1],
                                in_=smrow[0:1, ssub * 128:(ssub + 1) * 128],
                                identity=onesrf[0:1, 0:1]).then_inc(sP.h, 1)
                    t.wait_ge(sV.h, v_rc[h][pan][ssub])
                for ssub in range(4):
                    t.wait_ge(sV.h, v_rcp[h][pan])
                    if (h * NPAN + pan) * 4 + ssub >= 1:
                        pass
                    last = None
                    for zi in range(nzp(pan)):
                        last = t.matmul(pav[:],
                                        lhsT=ebuf[:, zi * 512 + ssub * 128:
                                                  zi * 512 + (ssub + 1) * 128],
                                        rhs=vtt[:, zi * 64:(zi + 1) * 64],
                                        start=(zi == 0), stop=(zi == nzp(pan) - 1))
                    last.then_inc(sP.h, 1)
                    t.wait_ge(sV.h, v_av[h][pan][ssub])
    p.add("tensor", t_all)

    p.run()
    return nc



_CACHE = {}


def _get(name, builder):
    if name not in _CACHE:
        _CACHE[name] = builder()
    return _CACHE[name]


def _run(nc, in_maps, outs):
    res = run_bass_kernel_spmd(nc, in_maps, core_ids=list(range(NCORES)))
    return [[res.results[c][o] for c in range(NCORES)] for o in outs]


def _prep_inputs(inp):
    ident = np.eye(128, dtype=np.float32)
    u = np.ascontiguousarray(inp["L0_attn_u"].reshape(8192, 2048))
    cast_src = dict(
        Win=inp["L0_attn_W_in"], Wout=inp["L0_attn_W_out"],
        expert=inp["L1_ffn_w_experts"].reshape(8 * 2048, 2048),
        cq=inp["L1_attn_w_cq"], qn=inp["L1_attn_w_q_nope"],
        qr=inp["L1_attn_w_q_rope"], ckv=inp["L1_attn_w_ckv"],
        kn=inp["L1_attn_w_k_nope"], vv=inp["L1_attn_w_v"],
        kr=inp["L1_attn_w_k_rope"], proj=inp["L1_attn_w_proj"],
    )
    rows = dict(Win=1044, Wout=256, expert=2048, cq=256, qn=96, qr=96,
                ckv=256, kn=32, vv=32, kr=256, proj=128)
    in_maps = []
    for c in range(NCORES):
        m = dict(
            w1s=np.ascontiguousarray(inp["L0_ffn_w1"][c * 704:(c + 1) * 704]),
            w2s=np.ascontiguousarray(inp["L0_ffn_w2"][c * 704:(c + 1) * 704]),
            w3s=np.ascontiguousarray(inp["L0_ffn_w3"][c * 256:(c + 1) * 256]),
            uslice=np.ascontiguousarray(u[c * 1024:(c + 1) * 1024]),
            router=np.ascontiguousarray(inp["L1_ffn_w_router"]),
            identf=ident,
        )
        for nm, arr in cast_src.items():
            r = rows[nm]
            m[nm + "_in"] = np.ascontiguousarray(np.asarray(arr, np.float32)[c * r:(c + 1) * r])
        in_maps.append(m)
    return in_maps


def run_prep(inp):
    nc = _get("prep", build_prep)
    res = run_bass_kernel_spmd(nc, _prep_inputs(inp), core_ids=list(range(NCORES)))
    r = res.results

    def cat(name):
        return np.concatenate([r[c][name] for c in range(NCORES)], axis=0)

    return dict(
        w1q=cat("w1q"), w2q=cat("w2q"), w3q=cat("w3q"),
        Win=cat("Win_bf"), Wout=cat("Wout_bf"),
        experts=np.ascontiguousarray(cat("expert_bf")),
        cq=cat("cq_bf"), qn=cat("qn_bf"), qr=cat("qr_bf"), ckv=cat("ckv_bf"),
        kn=cat("kn_bf"), vv=cat("vv_bf"), kr=cat("kr_bf"), proj=cat("proj_bf"),
        routerT=r[0]["routerT"],
        xst=[r[c]["xst0"] for c in range(NCORES)],
    )


def run_ffn(pp, xs, coef=0.5):
    nc = _get("ffn", lambda: build_ffn(coef))
    in_maps = [dict(xst=xs[c], w1q=pp["w1q"], w2q=pp["w2q"], w3q=pp["w3q"])
               for c in range(NCORES)]
    return _run(nc, in_maps, ["xout"])[0]


def run_moe(pp, xs, coef=0.5, final_out=False):
    key = "moef" if final_out else "moe"
    nc = _get(key, lambda: build_moe(coef, final_out))
    ident = np.eye(128, dtype=np.float32)
    in_maps = [dict(xst=xs[c], experts=pp["experts"], routerT=pp["routerT"],
                    identf=ident) for c in range(NCORES)]
    return _run(nc, in_maps, ["xout"])[0]


def run_m1(pp, xs):
    nc = _get("m1", build_m1)
    in_maps = [dict(xst=xs[c], winq=pp["Win"]) for c in range(NCORES)]
    z, xbc, dtr = _run(nc, in_maps, ["zps", "xbc", "dtr"])
    return z, xbc, dtr


def run_m2(pp, inp, xbc, dtr):
    nc = _get("m2", build_m2)
    cw = np.asarray(inp["L0_attn_conv_w"], np.float32)
    cb = np.asarray(inp["L0_attn_conv_b"], np.float32)
    dtb = np.asarray(inp["L0_attn_dt_bias"], np.float32)
    al = np.asarray(inp["L0_attn_A_log"], np.float32)
    dv = np.asarray(inp["L0_attn_D"], np.float32)
    eh = np.zeros((16, 2048), np.float32)
    for h in range(16):
        eh[h, h * 128:(h + 1) * 128] = 1.0
    m0 = np.where(np.arange(256)[None, :] >= np.arange(128)[:, None], 0.0, -1e30)
    m1m = np.where(np.arange(256)[None, :] >= (128 + np.arange(128))[:, None], 0.0, -1e30)
    l0m = np.concatenate([m0, m1m], axis=1).astype(np.float32)
    idb = np.eye(128, dtype=np.float32).astype(BF)
    idf = np.eye(128, dtype=np.float32)
    in_maps = []
    for c in range(NCORES):
        b, hh = c // 2, c % 2
        xb = np.concatenate([xbc[2 * b], xbc[2 * b + 1]], axis=1)
        dt2 = np.concatenate([dtr[2 * b], dtr[2 * b + 1]], axis=1)
        in_maps.append(dict(
            xpart=np.ascontiguousarray(xb[2048 * hh:2048 * hh + 2048]),
            bin=np.ascontiguousarray(xb[4096:4160]),
            cin=np.ascontiguousarray(xb[4160:4224]),
            dtrr=np.ascontiguousarray(dt2[16 * hh:16 * hh + 16]),
            convwx=np.ascontiguousarray(
                cw[2048 * hh:2048 * hh + 2048].reshape(16, 128, 4)
                .transpose(1, 0, 2).reshape(128, 64)),
            convwb=np.ascontiguousarray(
                cw[4096:4224].reshape(2, 64, 4).transpose(1, 0, 2).reshape(64, 8)),
            convbx=np.ascontiguousarray(
                cb[2048 * hh:2048 * hh + 2048].reshape(16, 128).T),
            convbb=np.ascontiguousarray(cb[4096:4224].reshape(2, 64).T),
            dtbias=np.ascontiguousarray(dtb[16 * hh:16 * hh + 16, None]),
            alog=np.ascontiguousarray(al[16 * hh:16 * hh + 16, None]),
            dvecr=np.ascontiguousarray(dv[None, 16 * hh:16 * hh + 16]),
            ehot=eh, l0mask=l0m, identb=idb, identf=idf,
        ))
    return _run(nc, in_maps, ["yss"])[0]


def run_m3(pp, inp, xs, zs, yss):
    nc = _get("m3", build_m3)
    nw = np.asarray(inp["L0_attn_norm_w"], np.float32)
    nwl = np.ascontiguousarray(nw.reshape(32, 128).T)
    in_maps = []
    for c in range(NCORES):
        b, half = c // 2, c % 2
        yin = np.ascontiguousarray(np.concatenate(
            [yss[2 * b][:, half * 1024:(half + 1) * 1024],
             yss[2 * b + 1][:, half * 1024:(half + 1) * 1024]], axis=0))
        in_maps.append(dict(xst=xs[c], yin=yin, zin=zs[c], nweight=nwl,
                            woutq=pp["Wout"]))
    return _run(nc, in_maps, ["xout"])[0]


def _costab():
    fr = 1.0 / 10000.0 ** (np.arange(0, 32, 2, dtype=np.float32) / 32.0)
    return np.cos(np.outer(np.arange(2048, dtype=np.float32), fr)).astype(np.float32)


def run_mla1(pp, inp, xs):
    nc = _get("mla1", build_mla1)
    cos = _costab()
    qg = np.ascontiguousarray(np.asarray(inp["L1_attn_q_g"], np.float32).reshape(6, 128).T)
    kvg = np.ascontiguousarray(np.asarray(inp["L1_attn_kv_g"], np.float32).reshape(2, 128).T)
    idb = np.eye(128, dtype=np.float32).astype(BF)
    in_maps = []
    for c in range(NCORES):
        half = c % 2
        ck = (cos[half * 1024:(half + 1) * 1024] / 16.0)
        ckl = np.ascontiguousarray(ck.reshape(8, 128, 16).transpose(1, 0, 2).reshape(128, 128))
        in_maps.append(dict(xst=xs[c], cqw=pp["cq"], ckvw=pp["ckv"], krw=pp["kr"],
                            qg=qg, kvg=kvg, costk=ckl, identb=idb))
    return _run(nc, in_maps, ["nq", "nkv", "krr"])


def run_mla2(pp, nq, nkv, krr):
    nc = _get("mla2", build_mla2)
    cos = _costab()
    s96 = np.float32(1.0 / np.sqrt(96.0))
    cq = np.ascontiguousarray((cos * s96).reshape(16, 128, 16).transpose(1, 0, 2).reshape(128, 256))
    dm = [(np.arange(512)[None, :] >= (v * 128 + np.arange(128))[:, None]
           ).astype(np.float32) for v in range(4)]
    dmk = np.concatenate(dm, axis=1).astype(BF)
    idb = np.eye(128, dtype=np.float32).astype(BF)
    in_maps = []
    for c in range(NCORES):
        b, g = c // 2, c % 2
        in_maps.append(dict(
            NQ=np.ascontiguousarray(np.concatenate([nq[2 * b], nq[2 * b + 1]], axis=1)),
            NKV=np.ascontiguousarray(np.concatenate([nkv[2 * b], nkv[2 * b + 1]], axis=1)),
            KRR=np.ascontiguousarray(np.concatenate([krr[2 * b], krr[2 * b + 1]], axis=1)),
            qnw=np.ascontiguousarray(pp["qn"][:, g * 512:(g + 1) * 512]),
            qrw=np.ascontiguousarray(pp["qr"][:, g * 256:(g + 1) * 256]),
            knw=np.ascontiguousarray(pp["kn"][:, g * 512:(g + 1) * 512]),
            vvw=np.ascontiguousarray(pp["vv"][:, g * 512:(g + 1) * 512]),
            costq=cq, dmask=dmk, identb=idb,
        ))
    return _run(nc, in_maps, ["OUT"])[0]


def run_mla3(pp, xs, outs):
    nc = _get("mla3", build_mla3)
    idb = np.eye(128, dtype=np.float32).astype(BF)
    in_maps = []
    for c in range(NCORES):
        b, half = c // 2, c % 2
        o0 = outs[2 * b].reshape(8, 2048, 64)[:, half * 1024:(half + 1) * 1024]
        o1 = outs[2 * b + 1].reshape(8, 2048, 64)[:, half * 1024:(half + 1) * 1024]
        outt = np.ascontiguousarray(
            np.concatenate([o0, o1], axis=0).transpose(1, 0, 2).reshape(1024, 1024))
        in_maps.append(dict(xst=xs[c], outt=outt, projw=pp["proj"], identb=idb))
    return _run(nc, in_maps, ["xout"])[0]


def kernel(**inputs):
    inp = {k: np.asarray(v) for k, v in inputs.items()}
    pp = run_prep(inp)
    xs = pp["xst"]
    xs = run_ffn(pp, xs, 0.5)
    zs, xbc, dtr = run_m1(pp, xs)
    yss = run_m2(pp, inp, xbc, dtr)
    xs = run_m3(pp, inp, xs, zs, yss)
    xs = run_ffn(pp, xs, 0.5)
    xs = run_moe(pp, xs, 0.5)
    nq, nkv, krr = run_mla1(pp, inp, xs)
    outs = run_mla2(pp, nq, nkv, krr)
    xs = run_mla3(pp, xs, outs)
    fin = run_moe(pp, xs, 0.5, final_out=True)
    out = np.stack(fin, axis=0).reshape(4, 2048, 2048)
    return np.ascontiguousarray(out.astype(np.float32))
